# version 03
# baseline (speedup 1.0000x reference)
import numpy as np
import ml_dtypes
import concourse.bass as bass
import concourse.mybir as mybir
from concourse.bass_utils import run_bass_kernel_spmd

F32 = mybir.dt.float32
BF = mybir.dt.bfloat16
AO = mybir.AluOpType
AF = mybir.ActivationFunctionType
AX = mybir.AxisListType

NCORES = 8
D = 2048
T = 8192
TMY = T // NCORES
DC = D // 128
NE = 16
S_ATT = float(1.0 / np.sqrt(128.0))
ALPHA = 1.702
LIMIT = 7.0
EPS = 1e-5
BIG = 1.0e30

KB = 1024

SEMS = [
    'sPE', 'sACT', 'sDVE',
    'sLDc',
    'sGLD',
    'sLDx0', 'sLDx1',
    'sSTc0', 'sSTc1',
    'sCC1',
    'sATT',
    'sXR0', 'sXR1',
    'sWo0', 'sWo1', 'sWo2', 'sWo3',
    'sSPL',
    'sXN',
    'sWa0', 'sWa1',
    'sWg0', 'sWg1',
    'sLD2',
    'sPOOL',
    'sCC2',
    'sATT2',
    'sSTy',
]


class Emitter:

    def __init__(self, nc, H, sel, eng):
        self.nc = nc
        self.H = H
        self.sel = sel
        self.eng = eng
        self.c = {s: 0 for s in SEMS}
        self._lastwait = {}

    def do(self, eng_name, fn, incs=(), waits=()):
        if self.sel == eng_name:
            for (s, v) in waits:
                if v <= 0:
                    continue
                if self._lastwait.get(s, -1) >= v:
                    continue
                self._lastwait[s] = v
                self.eng.wait_ge(self.H[s], v)
            ins = fn(self.eng)
            for (s, n) in incs:
                ins.then_inc(self.H[s], n)
        for (s, n) in incs:
            self.c[s] += n


def _emit_program(E):
    nc = E.nc
    H = E.H
    sel = E.sel

    sp_loads = [
        ('eye_f32', 'c_eye_f32'), ('ones_f32', 'c_ones_f32'),
        ('bqkv_c', 'i_bqkv'), ('bout_c', 'i_bout'), ('nw_c', 'i_nw'),
        ('b1a_c', 'i_b1a'), ('b1b_c', 'i_b1b'), ('gb', 'i_gb'),
    ]
    for dst, src in sp_loads:
        E.do('sync', lambda e, d=dst, s=src: e.dma_start(
            out=H[d].ap(), in_=H[s].ap()), incs=[('sLDc', 16)])
    n_sp_loads = E.c['sLDc']

    gp_loads = [
        ('eye_bf', 'c_eye_f32'), ('ones_bf', 'c_ones_f32'), ('gwT', 'i_gwT'),
        ('wqkv_sb', 'i_wqkv'),
    ]
    for dst, src in gp_loads:
        E.do('gpsimd', lambda e, d=dst, s=src: e.dma_start(
            out=H[d].ap(), in_=H[s].ap()), incs=[('sGLD', 16)])
    n_gp_loads = E.c['sGLD']

    E.do('vector', lambda e: e.memset(H['minis'].ap(), 0.0), incs=[('sDVE', 1)])
    E.do('vector', lambda e: e.memset(H['minis'][0:1, 0:1], EPS),
         incs=[('sDVE', 1)], waits=[('sDVE', E.c['sDVE'])])
    mini_ready = E.c['sDVE']

    NBLK = 64
    TB = 128
    pe_after_qkv = [0] * NBLK
    pe_after_pack = [0] * NBLK
    pe_after_vtr = [0] * NBLK
    dve_after_xcopy = [0] * NBLK
    dve_after_evac = [0] * NBLK
    dve_after_vcopy = [0] * NBLK

    for bi in range(NBLK):
        par = bi % 2
        pk = ['pC', 'pD'][par]
        sem_x = 'sLDx%d' % par
        waits = []
        if bi >= 2:
            waits.append(('sPE', pe_after_pack[bi - 2]))
        E.do('gpsimd', lambda e, b=bi, p=par: e.dma_start(
            out=H['xtok%d' % p].ap(),
            in_=H['i_x'][b * TB:(b + 1) * TB, :]),
            incs=[(sem_x, 16)], waits=waits)
        n_loads_par = (bi // 2) + 1

        waits = [(sem_x, 16 * n_loads_par)]
        if bi >= 2:
            waits.append(('sDVE', dve_after_vcopy[bi - 2]))
        if bi == 0:
            waits.append(('sGLD', n_gp_loads))
        for dc in range(DC):
            last = (dc == DC - 1)
            E.do('tensor', lambda e, d=dc, p=par, pkk=pk: e.transpose(
                H[pkk].ap().bitcast(BF)[:, d * 128:(d + 1) * 128],
                H['xtok%d' % p][:, d * 128:(d + 1) * 128],
                H['eye_bf'].ap()),
                incs=([('sPE', 1)] if last else []),
                waits=(waits if dc == 0 else []))
        pe_after_pack[bi] = E.c['sPE']

        waits = [('sPE', pe_after_pack[bi]), ('sDVE', E.c['sDVE'])]
        if bi >= 1:
            waits.append(('sPE', pe_after_qkv[bi - 1]))
        E.do('vector', lambda e, pkk=pk: e.tensor_copy(
            H['xT'].ap(), H[pkk].ap().bitcast(BF)),
            incs=[('sDVE', 1)], waits=waits)
        dve_after_xcopy[bi] = E.c['sDVE']

        waits = [('sDVE', dve_after_xcopy[bi])]
        if bi == 0:
            waits.append(('sGLD', n_gp_loads))
        if bi >= 1:
            waits.append(('sDVE', dve_after_evac[bi - 1]))
        first = True
        for j in range(6):
            for dc in range(DC):
                last = (j == 5 and dc == DC - 1)
                E.do('tensor', lambda e, jj=j, d=dc: e.matmul(
                    H['pA'][:, jj * TB:(jj + 1) * TB],
                    H['wqkv_sb'][:, (d * 6 + jj) * 128:(d * 6 + jj + 1) * 128],
                    H['xT'][:, d * TB:(d + 1) * TB],
                    start=(d == 0), stop=(d == DC - 1)),
                    incs=([('sPE', 1)] if last else []),
                    waits=(waits if first else []))
                first = False
        pe_after_qkv[bi] = E.c['sPE']

        waits = [('sPE', pe_after_qkv[bi]), ('sLDc', n_sp_loads),
                 ('sDVE', E.c['sDVE'])]
        dests = ['qA', 'qB', 'kA', 'kB']
        for j in range(6):
            last = (j == 5)

            def evac(e, jj=j, b=bi):
                if jj < 4:
                    out_ap = H[dests[jj]][:, b * TB:(b + 1) * TB]
                else:
                    out_ap = H['vstage'][:, (jj - 4) * TB:(jj - 3) * TB]
                return e.tensor_scalar(out_ap, H['pA'][:, jj * TB:(jj + 1) * TB],
                                       H['bqkv_c'][:, jj:jj + 1], None, AO.add)
            E.do('vector', evac,
                 incs=([('sDVE', 1)] if last else []),
                 waits=(waits if j == 0 else []))
        dve_after_evac[bi] = E.c['sDVE']

        waits = [('sDVE', dve_after_evac[bi]), ('sDVE', dve_after_xcopy[bi])]
        for h in range(2):
            last = (h == 1)
            E.do('tensor', lambda e, hh=h, pkk=pk: e.transpose(
                H[pkk].ap().bitcast(BF)[:, hh * 128:(hh + 1) * 128],
                H['vstage'][:, hh * 128:(hh + 1) * 128],
                H['eye_bf'].ap()),
                incs=([('sPE', 1)] if last else []),
                waits=(waits if h == 0 else []))
        pe_after_vtr[bi] = E.c['sPE']
        for h in range(2):
            E.do('vector', lambda e, hh=h, b=bi, pkk=pk: e.tensor_copy(
                H['vT%s' % 'AB'[hh]][:, b * TB:(b + 1) * TB],
                H[pkk].ap().bitcast(BF)[:, hh * 128:(hh + 1) * 128]),
                incs=([('sDVE', 1)] if h == 1 else []),
                waits=([('sPE', pe_after_vtr[bi]), ('sDVE', E.c['sDVE'])]
                       if h == 0 else []))
        dve_after_vcopy[bi] = E.c['sDVE']

    pe_p0_done = E.c['sPE']
    dve_p0_done = E.c['sDVE']

    wo_issued = [0, 0, 0, 0]

    def issue_wo(f, extra_waits=()):
        p4 = f % 4
        E.do('gpsimd', lambda e, ff=f, p=p4: e.dma_start(
            out=H['wo%d' % p].ap(), in_=H['i_wout'][ff, :, :]),
            incs=[('sWo%d' % p4, 16)], waits=list(extra_waits))
        wo_issued[p4] += 1

    for f in range(4):
        issue_wo(f, extra_waits=[('sPE', pe_p0_done)])

    NQB = T // 1024
    NKC = T // 128
    dve_after_attst = {}
    stc_count = [0, 0]
    last_exp_par = {0: 0, 1: 0}
    last_av_par = {0: 0, 1: 0}
    last_attst = 0
    sq_war = {0: 0, 1: 0}
    last_sig_dve = [0]
    last_sig_pool = [0]
    pe_sig_final = [0]

    for h in range(2):
        qn = 'q%s' % 'AB'[h]
        kn = 'k%s' % 'AB'[h]
        vn = 'vT%s' % 'AB'[h]

        for (src, mslot) in ((qn, 2), (kn, 3)):
            E.do('vector', lambda e, ms=mslot: e.memset(H['minis'][0:1, ms:ms + 1], 0.0),
                 incs=[('sDVE', 1)], waits=[('sDVE', E.c['sDVE'])])
            for j in range(16):
                p2 = j % 2
                E.do('scalar', lambda e, jj=j, p=p2, s=src: e.activation(
                    H['sqb'][:, p * 512:(p + 1) * 512],
                    H[s][:, jj * 512:(jj + 1) * 512], AF.Square),
                    incs=[('sACT', 1)],
                    waits=[('sDVE', dve_p0_done), ('sPE', sq_war[p2])])
                act_n = E.c['sACT']
                E.do('tensor', lambda e, p=p2: e.matmul(
                    H['pD'][0:1, p * 512:(p + 1) * 512],
                    H['ones_f32'][:, 0:1], H['sqb'][:, p * 512:(p + 1) * 512],
                    start=True, stop=True),
                    incs=[('sPE', 1)],
                    waits=[('sACT', act_n), ('sDVE', E.c['sDVE'])])
                sq_war[p2] = E.c['sPE']
                pe_n = E.c['sPE']
                E.do('vector', lambda e, p=p2: e.reduce_max(
                    H['minis'][0:1, 4:5], H['pD'][0:1, p * 512:(p + 1) * 512],
                    axis=AX.X),
                    incs=[('sDVE', 1)],
                    waits=[('sPE', pe_n), ('sDVE', E.c['sDVE'])])
                E.do('vector', lambda e, ms=mslot: e.tensor_tensor(
                    H['minis'][0:1, ms:ms + 1], H['minis'][0:1, ms:ms + 1],
                    H['minis'][0:1, 4:5], AO.max),
                    incs=[('sDVE', 1)], waits=[('sDVE', E.c['sDVE'])])
        E.do('scalar', lambda e: e.activation(
            H['minis'][0:1, 5:6], H['minis'][0:1, 2:3], AF.Sqrt),
            incs=[('sACT', 1)], waits=[('sDVE', E.c['sDVE'])])
        E.do('scalar', lambda e: e.activation(
            H['minis'][0:1, 6:7], H['minis'][0:1, 3:4], AF.Sqrt),
            incs=[('sACT', 1)], waits=[])
        E.do('vector', lambda e: e.tensor_tensor(
            H['minis'][0:1, 7:8], H['minis'][0:1, 5:6], H['minis'][0:1, 6:7],
            AO.mult),
            incs=[('sDVE', 1)],
            waits=[('sACT', E.c['sACT']), ('sDVE', E.c['sDVE'])])
        E.do('vector', lambda e: e.tensor_scalar(
            H['minis'][0:1, 8:9], H['minis'][0:1, 7:8], -S_ATT, None, AO.mult),
            incs=[('sDVE', 1)], waits=[('sDVE', E.c['sDVE'])])
        dven = E.c['sDVE']
        E.do('tensor', lambda e: e.matmul(
            H['pD'][:, 0:1], H['ones_f32'][0:1, :], H['minis'][0:1, 8:9],
            start=True, stop=True),
            incs=[('sPE', 1)], waits=[('sDVE', dven)])
        pen = E.c['sPE']
        E.do('vector', lambda e, hh=h: e.tensor_copy(
            H['cbias'][:, hh:hh + 1], H['pD'][:, 0:1]),
            incs=[('sDVE', 1)], waits=[('sPE', pen), ('sDVE', E.c['sDVE'])])
        dve_cbias = E.c['sDVE']

        for qb in range(NQB):
            act_exp_q = [0] * NKC

            def emit_scores(kc):
                spar = kc % 2
                pS = ['pA', 'pB'][spar]
                waits = [('sACT', last_exp_par[spar])]
                if qb == 0 and kc < 2:
                    waits.append(('sDVE', dve_cbias))
                for n in range(2):
                    E.do('tensor', lambda e, k=kc, nn=n, q=qb, ps=pS: e.matmul(
                        H[ps][:, nn * 512:(nn + 1) * 512],
                        H[kn][:, k * 128:(k + 1) * 128],
                        H[qn][:, q * 1024 + nn * 512: q * 1024 + (nn + 1) * 512],
                        start=True, stop=True),
                        incs=([('sPE', 1)] if n == 1 else []),
                        waits=(waits if n == 0 else []))
                pe_score = E.c['sPE']
                sig_w = (('sDVE', last_sig_dve[0]) if kc % 2 == 0
                         else ('sPOOL', last_sig_pool[0]))
                E.do('scalar', lambda e, k=kc, ps=pS, hh=h: e.activation(
                    H['E%d' % (k % 2)].ap(), H[ps].ap(),
                    AF.Exp, bias=H['cbias'][:, hh:hh + 1], scale=S_ATT),
                    incs=[('sACT', 1)],
                    waits=[('sPE', pe_score), ('sPE', last_av_par[kc % 2]),
                           sig_w])
                last_exp_par[spar] = E.c['sACT']
                act_exp_q[kc] = E.c['sACT']

            def emit_av(kc):
                waits = [('sACT', act_exp_q[kc])]
                if kc == 0:
                    waits.append(('sDVE', last_attst))
                for n in range(2):
                    last = (n == 1)
                    E.do('tensor', lambda e, k=kc, nn=n: e.matmul(
                        H['pC'][:, nn * 512:(nn + 1) * 512],
                        H[vn][:, k * 128:(k + 1) * 128],
                        H['E%d' % (k % 2)][:, nn * 512:(nn + 1) * 512],
                        start=(k == 0), stop=(k == NKC - 1)),
                        incs=([('sPE', 1)] if last else []),
                        waits=(waits if n == 0 else []))
                last_av_par[kc % 2] = E.c['sPE']

            def emit_sig(kc):
                eng = 'vector' if kc % 2 == 0 else 'gpsimd'
                sem = 'sDVE' if kc % 2 == 0 else 'sPOOL'
                acc = 'sacc_d' if kc % 2 == 0 else 'sacc_p'
                waits = [('sACT', act_exp_q[kc]), (sem, E.c[sem])]
                if kc < 2:
                    waits.append(('sPE', pe_sig_final[0]))
                if kc % 2 == 0:
                    E.do(eng, lambda e, k=kc, a=acc: (
                        e.tensor_copy(H[a].ap(), H['E%d' % (k % 2)].ap())
                        if k < 2 else
                        e.tensor_tensor(H[a].ap(), H[a].ap(),
                                        H['E%d' % (k % 2)].ap(), AO.add)),
                        incs=[(sem, 1)], waits=waits)
                else:
                    E.do(eng, lambda e, k=kc, a=acc: (
                        e.tensor_copy(H[a].ap(), H['E%d' % (k % 2)].ap())
                        if k < 2 else
                        e.tensor_tensor(H[a].ap(), H[a].ap(),
                                        H['E%d' % (k % 2)].ap(), AO.add)),
                        incs=[(sem, 1)], waits=waits)
                if kc % 2 == 0:
                    last_sig_dve[0] = E.c[sem]
                else:
                    last_sig_pool[0] = E.c[sem]

            emit_scores(0)
            for kc in range(1, NKC):
                emit_scores(kc)
                emit_av(kc - 1)
                emit_sig(kc - 1)
            emit_av(NKC - 1)
            emit_sig(NKC - 1)
            pe_av_last = E.c['sPE']

            waits = [('sDVE', last_sig_dve[0]), ('sPOOL', last_sig_pool[0]),
                     ('sDVE', last_attst)]
            k2 = 0
            for acc in ('sacc_d', 'sacc_p'):
                for n in range(2):
                    lastmm = (k2 == 3)
                    E.do('tensor', lambda e, a=acc, nn=n: e.matmul(
                        H['pD'][0:1, nn * 512:(nn + 1) * 512],
                        H['ones_f32'][:, 0:1],
                        H[a][:, nn * 512:(nn + 1) * 512],
                        start=(a == 'sacc_d'), stop=(a == 'sacc_p')),
                        incs=([('sPE', 1)] if lastmm else []),
                        waits=(waits if k2 == 0 else []))
                    k2 += 1
            pe_sig_final[0] = E.c['sPE']

            E.do('vector', lambda e: e.tensor_copy(H['r0'].ap(), H['pD'][0:1, 0:1024]),
                 incs=[('sDVE', 1)],
                 waits=[('sPE', pe_sig_final[0]), ('sPE', pe_av_last),
                        ('sDVE', E.c['sDVE'])])
            E.do('vector', lambda e: e.reciprocal(H['r1'].ap(), H['r0'].ap()),
                 incs=[('sDVE', 1)], waits=[('sDVE', E.c['sDVE'])])
            dven = E.c['sDVE']
            for n in range(2):
                E.do('tensor', lambda e, nn=n: e.matmul(
                    H['pD'][:, nn * 512:(nn + 1) * 512],
                    H['ones_f32'][0:1, :], H['r1'][0:1, nn * 512:(nn + 1) * 512],
                    start=True, stop=True),
                    incs=([('sPE', 1)] if n == 1 else []),
                    waits=([('sDVE', dven)] if n == 0 else []))
            pen = E.c['sPE']
            E.do('vector', lambda e: e.tensor_copy(H['rbc'].ap(), H['pD'].ap()),
                 incs=[('sDVE', 1)], waits=[('sPE', pen), ('sDVE', E.c['sDVE'])])
            apar = (h * NQB + qb) % 2
            E.do('vector', lambda e, ap_=apar: e.tensor_tensor(
                H['att%d' % ap_].ap(), H['pC'].ap(), H['rbc'].ap(), AO.mult),
                incs=[('sDVE', 1)],
                waits=[('sDVE', E.c['sDVE']),
                       ('sSTc%d' % apar, 16 * stc_count[apar])])
            dve_after_attst[(h, qb)] = E.c['sDVE']
            last_attst = E.c['sDVE']
            E.do('sync', lambda e, hh=h, q=qb, ap_=apar: e.dma_start(
                out=H['cc_in'][hh * 128:(hh + 1) * 128, q * 1024:(q + 1) * 1024],
                in_=H['att%d' % ap_].ap()),
                incs=[('sSTc%d' % apar, 16)],
                waits=[('sDVE', dve_after_attst[(h, qb)])])
            stc_count[apar] += 1

        E.do('gpsimd', lambda e, hh=h: e.collective_compute(
            'AllGather', AO.bypass, replica_groups=[list(range(NCORES))],
            ins=[H['cc_in'][hh * 128:(hh + 1) * 128, :].opt()],
            outs=[H['cc_out%d' % hh].ap().opt()]),
            incs=[('sCC%d' % (h + 1), 1)],
            waits=[('sSTc0', 16 * stc_count[0]), ('sSTc1', 16 * stc_count[1])])

    pe_p1_done = E.c['sPE']

    if sel == 'sync':
        pid = nc.partition_id()
        att_off = pid * TMY
    for hh in range(2):
        sematt = ['sATT', 'sATT2'][hh]
        for r in range(NCORES):
            pos = hh * 8 + r

            def ld(e, rr=r, p=pos, hx=hh):
                sym = bass.AP(H['cc_out%d' % hx], rr * 128 * T + att_off,
                              [[T, 128], [1, TMY]])
                return e.dma_start(
                    out=H['attn_all'][:, p * TMY:(p + 1) * TMY], in_=sym)
            E.do('sync', ld, incs=[(sematt, 16)],
                 waits=[('sCC%d' % (hh + 1), 1), ('sPE', pe_p1_done)])
    n_att = E.c['sATT']
    n_att2 = E.c['sATT2']

    def attn_pos(dc):
        return (dc % 2) * 8 + dc // 2

    xr_issued = [0, 0]

    def issue_xr(r, waits=()):
        p2 = r % 2
        E.do('sync', lambda e, rr=r, p=p2: e.dma_start(
            out=H['xrel%d' % p].ap(), in_=H['i_xmy'][rr * 128:(rr + 1) * 128, :]),
            incs=[('sXR%d' % p2, 16)], waits=list(waits))
        xr_issued[p2] += 1

    issue_xr(0)
    issue_xr(1)

    pe_after_xtr = [0] * 8
    dve_after_xfm = [0] * 8
    for r in range(8):
        p2 = r % 2
        for half in range(2):
            pk = ['pC', 'pD'][half]
            waits = [('sXR%d' % p2, 16 * ((r // 2) + 1))]
            if r == 0 and half == 0:
                waits.append(('sPE', pe_p1_done))
            if r >= 1:
                waits.append(('sDVE', dve_after_xfm[r - 1]))
            for i in range(8):
                dc = half * 8 + i
                last = (i == 7)
                E.do('tensor', lambda e, d=dc, p=p2, pkk=pk, ii=i: e.transpose(
                    H[pkk][:, ii * 128:(ii + 1) * 128],
                    H['xrel%d' % p][:, d * 128:(d + 1) * 128],
                    H['eye_f32'].ap()),
                    incs=([('sPE', 1)] if last else []),
                    waits=(waits if i == 0 else []))
            pen = E.c['sPE']
            E.do('vector', lambda e, rr=r, hf=half, pkk=pk: e.tensor_copy(
                H['x_fm'].ap().rearrange('p (d m) -> p d m', d=DC)[
                    :, hf * 8:(hf + 1) * 8, rr * 128:(rr + 1) * 128],
                H[pkk].ap().rearrange('p (i m) -> p i m', i=8)),
                incs=[('sDVE', 1)], waits=[('sPE', pen), ('sDVE', E.c['sDVE'])])
        pe_after_xtr[r] = E.c['sPE']
        dve_after_xfm[r] = E.c['sDVE']
        if r + 2 < 8:
            issue_xr(r + 2, waits=[('sPE', pe_after_xtr[r])])

    pe_after_proj = [0] * DC
    dve_after_x1 = [0] * DC
    dc_order = [2 * r for r in range(8)] + [2 * r + 1 for r in range(8)]
    for f in range(DC):
        p4 = f % 4
        pP = ['pA', 'pB'][f % 2]
        waits = [('sWo%d' % p4, 16 * ((f // 4) + 1)), ('sATT', n_att),
                 ('sDVE', dve_after_xfm[7])]
        if f >= 2:
            waits.append(('sDVE', dve_after_x1[f - 2]))
        for di, dc in enumerate(dc_order):
            xw = [('sATT2', n_att2)] if di == 8 else []
            for n in range(2):
                last = (di == DC - 1 and n == 1)
                E.do('tensor', lambda e, d=dc, nn=n, ps=pP, p=p4: e.matmul(
                    H[ps][:, nn * 512:(nn + 1) * 512],
                    H['wo%d' % p][:, d * 128:(d + 1) * 128],
                    H['attn_all'][:, attn_pos(d) * TMY + nn * 512:
                                  attn_pos(d) * TMY + (nn + 1) * 512],
                    start=(di == 0), stop=(di == DC - 1)),
                    incs=([('sPE', 1)] if last else []),
                    waits=((waits if (di == 0 and n == 0) else []) +
                           (xw if n == 0 else [])))
        pe_after_proj[f] = E.c['sPE']
        if f + 4 < DC:
            issue_wo(f + 4, extra_waits=[('sPE', pe_after_proj[f])])
        E.do('vector', lambda e, ff=f, ps=pP: e.scalar_tensor_tensor(
            H['x_fm'][:, ff * TMY:(ff + 1) * TMY],
            H[ps].ap(), H['bout_c'][:, ff:ff + 1],
            H['x_fm'][:, ff * TMY:(ff + 1) * TMY],
            AO.add, AO.add),
            incs=[('sDVE', 1)],
            waits=[('sPE', pe_after_proj[f]), ('sDVE', E.c['sDVE'])])
        dve_after_x1[f] = E.c['sDVE']

    pe_stats1 = 0
    for f in range(DC):
        E.do('scalar', lambda e, ff=f: e.activation(
            H['sqb'].ap(), H['x_fm'][:, ff * TMY:(ff + 1) * TMY], AF.Square),
            incs=[('sACT', 1)],
            waits=[('sDVE', dve_after_x1[f]), ('sPE', E.c['sPE'])])
        act_n = E.c['sACT']
        for n in range(2):
            E.do('tensor', lambda e, ff=f, nn=n: e.matmul(
                H['pD'][0:1, nn * 512:(nn + 1) * 512], H['ones_f32'][:, 0:1],
                H['x_fm'][:, ff * TMY + nn * 512: ff * TMY + (nn + 1) * 512],
                start=(ff == 0), stop=(ff == DC - 1)),
                incs=[], waits=[('sDVE', dve_after_x1[f])])
        for n in range(2):
            last = (n == 1)
            E.do('tensor', lambda e, ff=f, nn=n: e.matmul(
                H['pC'][0:1, nn * 512:(nn + 1) * 512], H['ones_f32'][:, 0:1],
                H['sqb'][:, nn * 512:(nn + 1) * 512],
                start=(ff == 0), stop=(ff == DC - 1)),
                incs=([('sPE', 1)] if last else []),
                waits=[('sACT', act_n)])
    pe_stats1 = E.c['sPE']

    def ln_finalize(first_waits):
        E.do('vector', lambda e: e.tensor_scalar(
            H['r0'].ap(), H['pD'][0:1, 0:TMY], 1.0 / D, None, AO.mult),
            incs=[('sDVE', 1)],
            waits=list(first_waits) + [('sDVE', E.c['sDVE'])])
        E.do('vector', lambda e: e.tensor_scalar(
            H['r1'].ap(), H['pC'][0:1, 0:TMY], 1.0 / D, None, AO.mult),
            incs=[('sDVE', 1)], waits=[('sDVE', E.c['sDVE'])])
        E.do('scalar', lambda e: e.activation(H['r2'].ap(), H['r0'].ap(), AF.Square),
             incs=[('sACT', 1)], waits=[('sDVE', E.c['sDVE'])])
        E.do('vector', lambda e: e.tensor_tensor(
            H['r1'].ap(), H['r1'].ap(), H['r2'].ap(), AO.subtract),
            incs=[('sDVE', 1)],
            waits=[('sACT', E.c['sACT']), ('sDVE', E.c['sDVE'])])
        E.do('scalar', lambda e: e.activation(
            H['r2'].ap(), H['r1'].ap(), AF.Sqrt, bias=H['minis'][0:1, 0:1]),
            incs=[('sACT', 1)], waits=[('sDVE', E.c['sDVE'])])
        E.do('vector', lambda e: e.reciprocal(H['r3'].ap(), H['r2'].ap()),
             incs=[('sDVE', 1)],
             waits=[('sACT', E.c['sACT']), ('sDVE', E.c['sDVE'])])
        E.do('vector', lambda e: e.scalar_tensor_tensor(
            H['r1'].ap(), H['r0'].ap(), -1.0, H['r3'].ap(), AO.mult, AO.mult),
            incs=[('sDVE', 1)], waits=[('sDVE', E.c['sDVE'])])
        dven = E.c['sDVE']
        for (row, bc) in (('r3', 'bcA'), ('r1', 'bcB')):
            for n in range(2):
                E.do('tensor', lambda e, rr=row, nn=n: e.matmul(
                    H['pD'][:, nn * 512:(nn + 1) * 512], H['ones_f32'][0:1, :],
                    H[rr][0:1, nn * 512:(nn + 1) * 512], start=True, stop=True),
                    incs=([('sPE', 1)] if n == 1 else []),
                    waits=([('sDVE', dven), ('sDVE', E.c['sDVE'])] if n == 0 else []))
            pen2 = E.c['sPE']
            E.do('vector', lambda e, b=bc: e.tensor_copy(H[b].ap(), H['pD'].ap()),
                 incs=[('sDVE', 1)],
                 waits=[('sPE', pen2), ('sDVE', E.c['sDVE'])])
        return E.c['sDVE']

    dve_ln1 = ln_finalize([('sPE', pe_stats1)])

    dve_after_xn = [0] * DC
    for f in range(DC):
        E.do('vector', lambda e, ff=f: e.tensor_tensor(
            H['x_fm'][:, ff * TMY:(ff + 1) * TMY],
            H['x_fm'][:, ff * TMY:(ff + 1) * TMY], H['bcA'].ap(), AO.mult),
            incs=[('sDVE', 1)],
            waits=[('sDVE', dve_ln1), ('sDVE', E.c['sDVE'])])
        E.do('vector', lambda e, ff=f: e.tensor_tensor(
            H['x_fm'][:, ff * TMY:(ff + 1) * TMY],
            H['x_fm'][:, ff * TMY:(ff + 1) * TMY], H['bcB'].ap(), AO.add),
            incs=[('sDVE', 1)], waits=[('sDVE', E.c['sDVE'])])
        dve_after_xn[f] = E.c['sDVE']
    for f in range(DC):
        E.do('scalar', lambda e, ff=f: e.activation(
            H['sqb'].ap(), H['x_fm'][:, ff * TMY:(ff + 1) * TMY], AF.Square),
            incs=[('sACT', 1)],
            waits=[('sDVE', dve_after_xn[f]), ('sPE', E.c['sPE'])])
        act_n = E.c['sACT']
        for n in range(2):
            last = (n == 1)
            E.do('tensor', lambda e, ff=f, nn=n: e.matmul(
                H['pC'][0:1, nn * 512:(nn + 1) * 512], H['ones_f32'][:, 0:1],
                H['sqb'][:, nn * 512:(nn + 1) * 512],
                start=(ff == 0), stop=(ff == DC - 1)),
                incs=([('sPE', 1)] if last else []),
                waits=[('sACT', act_n)])
    pe_rms = E.c['sPE']

    E.do('vector', lambda e: e.tensor_scalar(
        H['r1'].ap(), H['pC'][0:1, 0:TMY], 1.0 / D, None, AO.mult),
        incs=[('sDVE', 1)],
        waits=[('sPE', pe_rms), ('sDVE', E.c['sDVE'])])
    E.do('scalar', lambda e: e.activation(
        H['r2'].ap(), H['r1'].ap(), AF.Sqrt, bias=H['minis'][0:1, 0:1]),
        incs=[('sACT', 1)], waits=[('sDVE', E.c['sDVE'])])
    E.do('vector', lambda e: e.reciprocal(H['r3'].ap(), H['r2'].ap()),
         incs=[('sDVE', 1)],
         waits=[('sACT', E.c['sACT']), ('sDVE', E.c['sDVE'])])
    dven = E.c['sDVE']
    for n in range(2):
        E.do('tensor', lambda e, nn=n: e.matmul(
            H['pD'][:, nn * 512:(nn + 1) * 512], H['ones_f32'][0:1, :],
            H['r3'][0:1, nn * 512:(nn + 1) * 512], start=True, stop=True),
            incs=([('sPE', 1)] if n == 1 else []),
            waits=([('sDVE', dven)] if n == 0 else []))
    pen = E.c['sPE']
    E.do('vector', lambda e: e.tensor_copy(H['bcC'].ap(), H['pD'].ap()),
         incs=[('sDVE', 1)], waits=[('sPE', pen), ('sDVE', E.c['sDVE'])])
    dve_bcC = E.c['sDVE']

    dve_after_t = [0] * DC
    for f in range(DC):
        E.do('vector', lambda e, ff=f: e.tensor_tensor(
            H['sqb'].ap(), H['x_fm'][:, ff * TMY:(ff + 1) * TMY], H['bcC'].ap(),
            AO.mult),
            incs=[('sDVE', 1)],
            waits=[('sDVE', dve_bcC), ('sDVE', E.c['sDVE']),
                   ('sPE', pe_after_proj[DC - 1])])
        E.do('vector', lambda e, ff=f: e.tensor_scalar(
            H['t_sb'][:, ff * TMY:(ff + 1) * TMY], H['sqb'].ap(),
            H['nw_c'][:, ff:ff + 1], None, AO.mult),
            incs=[('sDVE', 1)], waits=[('sDVE', E.c['sDVE'])])
        dve_after_t[f] = E.c['sDVE']
    dve_t_done = dve_after_t[DC - 1]

    E.do('sync', lambda e: e.dma_start(out=H['xn_spill'].ap(), in_=H['x_fm'].ap()),
         incs=[('sSPL', 16)], waits=[('sDVE', dve_t_done)])

    E.do('sync', lambda e: e.dma_start(out=H['b2_sb'].ap(), in_=H['i_b2'].ap()),
         incs=[('sLD2', 16)], waits=[])
    E.do('sync', lambda e: e.dma_start(out=H['oneh'].ap(), in_=H['i_oneh'].ap()),
         incs=[('sLD2', 16)], waits=[('sDVE', dve_t_done)])
    n_ld2 = E.c['sLD2']

    for c in range(8):
        waits = [('sDVE', dve_t_done), ('sGLD', n_gp_loads)]
        for dc in range(DC):
            E.do('tensor', lambda e, cc=c, d=dc: e.matmul(
                H['pD'][:, cc * 16:(cc + 1) * 16],
                H['t_sb'][:, d * TMY + cc * 128: d * TMY + (cc + 1) * 128],
                H['gwT'][:, d * 16:(d + 1) * 16],
                start=(d == 0), stop=False),
                incs=[], waits=(waits if (dc == 0 and c == 0) else []))
        last = (c == 7)
        E.do('tensor', lambda e, cc=c: e.matmul(
            H['pD'][:, cc * 16:(cc + 1) * 16],
            H['ones_f32'][0:1, :], H['gb'].ap(),
            start=False, stop=True),
            incs=([('sPE', 1)] if last else []),
            waits=([('sLDc', n_sp_loads)] if c == 0 else []))
    pe_logits = E.c['sPE']
    E.do('vector', lambda e: e.tensor_copy(H['G_sb'].ap(), H['pD'][:, 0:128]),
         incs=[('sDVE', 1)], waits=[('sPE', pe_logits), ('sDVE', E.c['sDVE'])])

    for c in range(8):
        Lsl = (c * 16, (c + 1) * 16)
        E.do('vector', lambda e, s=Lsl: e.tensor_copy(
            H['Ltmp'].ap(), H['G_sb'][:, s[0]:s[1]]),
            incs=[('sDVE', 1)], waits=[('sDVE', E.c['sDVE'])])
        for it in range(4):
            E.do('vector', lambda e, sl=it: e.reduce_max(
                H['mq4'][:, sl:sl + 1], H['Ltmp'].ap(), axis=AX.X),
                incs=[('sDVE', 1)], waits=[('sDVE', E.c['sDVE'])])
            if it < 3:
                E.do('vector', lambda e, sl=it: e.tensor_scalar(
                    H['eqm'].ap(), H['Ltmp'].ap(), H['mq4'][:, sl:sl + 1], None,
                    AO.is_equal),
                    incs=[('sDVE', 1)], waits=[('sDVE', E.c['sDVE'])])
                E.do('vector', lambda e: e.scalar_tensor_tensor(
                    H['Ltmp'].ap(), H['eqm'].ap(), -BIG, H['Ltmp'].ap(),
                    AO.mult, AO.add),
                    incs=[('sDVE', 1)], waits=[('sDVE', E.c['sDVE'])])
        E.do('vector', lambda e, s=Lsl: e.tensor_scalar(
            H['eqm'].ap(), H['G_sb'][:, s[0]:s[1]], H['mq4'][:, 3:4], None,
            AO.is_ge),
            incs=[('sDVE', 1)], waits=[('sDVE', E.c['sDVE'])])
        E.do('vector', lambda e: e.tensor_scalar(
            H['mq4'][:, 4:5], H['mq4'][:, 0:1], -1.0, None, AO.mult),
            incs=[('sDVE', 1)], waits=[('sDVE', E.c['sDVE'])])
        dven = E.c['sDVE']
        E.do('scalar', lambda e, s=Lsl: e.activation(
            H['expv'].ap(), H['G_sb'][:, s[0]:s[1]], AF.Exp,
            bias=H['mq4'][:, 4:5]),
            incs=[('sACT', 1)], waits=[('sDVE', dven)])
        E.do('vector', lambda e: e.tensor_tensor(
            H['expv'].ap(), H['expv'].ap(), H['eqm'].ap(), AO.mult),
            incs=[('sDVE', 1)],
            waits=[('sACT', E.c['sACT']), ('sDVE', E.c['sDVE'])])
        E.do('vector', lambda e: e.reduce_sum(
            H['mq4'][:, 5:6], H['expv'].ap(), axis=AX.X),
            incs=[('sDVE', 1)], waits=[('sDVE', E.c['sDVE'])])
        E.do('vector', lambda e: e.reciprocal(H['mq4'][:, 6:7], H['mq4'][:, 5:6]),
             incs=[('sDVE', 1)], waits=[('sDVE', E.c['sDVE'])])
        E.do('vector', lambda e, s=Lsl: e.tensor_scalar(
            H['G_sb'][:, s[0]:s[1]], H['expv'].ap(), H['mq4'][:, 6:7], None,
            AO.mult),
            incs=[('sDVE', 1)], waits=[('sDVE', E.c['sDVE'])])
    dve_G = E.c['sDVE']

    for c in range(8):
        last = (c == 7)
        E.do('tensor', lambda e, cc=c: e.transpose(
            H['pC'][0:16, cc * 128:(cc + 1) * 128],
            H['G_sb'][:, cc * 16:(cc + 1) * 16], H['eye_f32'].ap()),
            incs=([('sPE', 1)] if last else []),
            waits=([('sDVE', dve_G)] if c == 0 else []))
    pen = E.c['sPE']
    E.do('vector', lambda e: e.tensor_copy(H['GT_sb'].ap(), H['pC'][0:16, 0:TMY]),
         incs=[('sDVE', 1)], waits=[('sPE', pen), ('sDVE', E.c['sDVE'])])
    dve_GT = E.c['sDVE']

    dve_after_dinit = [0] * DC
    for g in range(DC):
        pP = ['pA', 'pB'][g % 2]
        waits = [('sDVE', dve_GT), ('sLD2', n_ld2), ('sSPL', 16)]
        if g >= 2:
            waits.append(('sDVE', dve_after_dinit[g - 2]))
        for n in range(2):
            E.do('tensor', lambda e, gg=g, nn=n, ps=pP: e.matmul(
                H[ps][:, nn * 512:(nn + 1) * 512],
                H['b2_sb'][:, gg * 128:(gg + 1) * 128],
                H['GT_sb'][0:16, nn * 512:(nn + 1) * 512],
                start=True, stop=True),
                incs=([('sPE', 1)] if n == 1 else []),
                waits=(waits if n == 0 else []))
        pen = E.c['sPE']
        E.do('vector', lambda e, gg=g, ps=pP: e.tensor_copy(
            H['delta'][:, gg * TMY:(gg + 1) * TMY], H[ps].ap()),
            incs=[('sDVE', 1)], waits=[('sPE', pen), ('sDVE', E.c['sDVE'])])
        dve_after_dinit[g] = E.c['sDVE']

    wa_issued = [0, 0]
    wg_issued = [0, 0]
    pe_after_h = {}
    pe_after_w2 = {}
    dve_after_act = {}
    dve_after_dacc = {}
    pair_last = {0: dve_after_dinit[DC - 1], 1: dve_GT}

    def issue_wa(e_, c_, waits=()):
        p2 = (e_ * DC + c_) % 2
        E.do('gpsimd', lambda e, ee=e_, cc=c_, p=p2: e.dma_start(
            out=H['wa%d' % p].ap(), in_=H['i_w1a'][ee, cc, :, :]),
            incs=[('sWa%d' % p2, 16)], waits=list(waits))
        E.do('gpsimd', lambda e, ee=e_, cc=c_, p=p2: e.dma_start(
            out=H['wb%d' % p].ap(), in_=H['i_w1b'][ee, cc, :, :]),
            incs=[('sWa%d' % p2, 16)])
        wa_issued[p2] += 1

    def issue_wg(e_, g_, waits=()):
        p2 = (e_ * DC + g_) % 2
        E.do('gpsimd', lambda e, ee=e_, gg=g_, p=p2: e.dma_start(
            out=H['wg%d' % p].ap(), in_=H['i_w2'][ee, gg, :, :]),
            incs=[('sWg%d' % p2, 16)], waits=list(waits))
        wg_issued[p2] += 1

    w_init_waits = [('sPE', pe_after_proj[DC - 1]), ('sDVE', dve_GT),
                    ('sACT', E.c['sACT'])]
    issue_wa(0, 0, waits=w_init_waits)
    issue_wa(0, 1)
    issue_wg(0, 0, waits=w_init_waits)
    issue_wg(0, 1)

    for e_ in range(NE):
        waits = [('sDVE', dve_GT), ('sLD2', n_ld2), ('sDVE', pair_last[1])]
        for n in range(2):
            E.do('tensor', lambda e, ee=e_, nn=n: e.matmul(
                H['pD'][:, nn * 512:(nn + 1) * 512],
                H['oneh'][0:16, ee * 128:(ee + 1) * 128],
                H['GT_sb'][0:16, nn * 512:(nn + 1) * 512],
                start=True, stop=True),
                incs=([('sPE', 1)] if n == 1 else []),
                waits=(waits if n == 0 else []))
        pen = E.c['sPE']
        E.do('vector', lambda e: e.tensor_copy(H['Gbc'].ap(), H['pD'].ap()),
             incs=[('sDVE', 1)], waits=[('sPE', pen), ('sDVE', E.c['sDVE'])])
        pair_last[1] = E.c['sDVE']
        dve_gbc = E.c['sDVE']

        for c in range(DC):
            gi = e_ * DC + c
            p2 = gi % 2
            pair = c % 2
            pha = ['pA', 'pC'][pair]
            phb = ['pB', 'pD'][pair]
            waits = [('sWa%d' % p2, 32 * ((gi // 2) + 1)),
                     ('sDVE', dve_t_done),
                     ('sDVE', pair_last[pair]),
                     ('sDVE', dve_gbc)]
            for dc in range(DC):
                for n in range(2):
                    E.do('tensor', lambda e, d=dc, nn=n, p=p2, ph=pha: e.matmul(
                        H[ph][:, nn * 512:(nn + 1) * 512],
                        H['wa%d' % p][:, d * 128:(d + 1) * 128],
                        H['t_sb'][:, d * TMY + nn * 512: d * TMY + (nn + 1) * 512],
                        start=(d == 0), stop=(d == DC - 1)),
                        incs=[], waits=(waits if (dc == 0 and n == 0) else []))
            for dc in range(DC):
                for n in range(2):
                    last = (dc == DC - 1 and n == 1)
                    E.do('tensor', lambda e, d=dc, nn=n, p=p2, ph=phb: e.matmul(
                        H[ph][:, nn * 512:(nn + 1) * 512],
                        H['wb%d' % p][:, d * 128:(d + 1) * 128],
                        H['t_sb'][:, d * TMY + nn * 512: d * TMY + (nn + 1) * 512],
                        start=(d == 0), stop=(d == DC - 1)),
                        incs=([('sPE', 1)] if last else []), waits=[])
            pe_after_h[gi] = E.c['sPE']
            ngi = gi + 2
            if ngi < NE * DC:
                issue_wa(ngi // DC, ngi % DC, waits=[('sPE', pe_after_h[gi])])

            bcol = e_ * 16 + c
            E.do('vector', lambda e, ph=pha, bc=bcol: e.tensor_scalar(
                H['ew_a'].ap(), H[ph].ap(), H['b1a_c'][:, bc:bc + 1], LIMIT,
                AO.add, AO.min),
                incs=[('sDVE', 1)],
                waits=[('sPE', pe_after_h[gi]), ('sDVE', E.c['sDVE'])])
            dven = E.c['sDVE']
            E.do('scalar', lambda e: e.activation(
                H['ew_s'].ap(), H['ew_a'].ap(), AF.Sigmoid, scale=ALPHA),
                incs=[('sACT', 1)], waits=[('sDVE', dven)])
            E.do('vector', lambda e: e.tensor_tensor(
                H['ew_a'].ap(), H['ew_a'].ap(), H['ew_s'].ap(), AO.mult),
                incs=[('sDVE', 1)],
                waits=[('sACT', E.c['sACT']), ('sDVE', E.c['sDVE'])])
            E.do('vector', lambda e, ph=phb, bc=bcol: e.tensor_scalar(
                H['ew_s'].ap(), H[ph].ap(), H['b1b_c'][:, bc:bc + 1], LIMIT,
                AO.add, AO.min),
                incs=[('sDVE', 1)], waits=[('sDVE', E.c['sDVE'])])
            E.do('vector', lambda e: e.tensor_scalar(
                H['ew_s'].ap(), H['ew_s'].ap(), -LIMIT, 1.0, AO.max, AO.add),
                incs=[('sDVE', 1)], waits=[('sDVE', E.c['sDVE'])])
            E.do('vector', lambda e: e.tensor_tensor(
                H['ew_s'].ap(), H['ew_s'].ap(), H['Gbc'].ap(), AO.mult),
                incs=[('sDVE', 1)], waits=[('sDVE', E.c['sDVE'])])
            E.do('vector', lambda e, cc=c: e.tensor_tensor(
                H['actp'][:, cc * TMY:(cc + 1) * TMY],
                H['ew_a'].ap(), H['ew_s'].ap(), AO.mult),
                incs=[('sDVE', 1)], waits=[('sDVE', E.c['sDVE'])])
            dve_after_act[gi] = E.c['sDVE']
            pair_last[pair] = E.c['sDVE']

        for g in range(DC):
            gi = e_ * DC + g
            p2 = gi % 2
            pP = ['pA', 'pB'][g % 2]
            waits = [('sWg%d' % p2, 16 * ((gi // 2) + 1)),
                     ('sDVE', dve_after_act[e_ * DC + DC - 1]),
                     ('sDVE', pair_last[0])]
            if g >= 2:
                waits.append(('sDVE', dve_after_dacc[gi - 2]))
            for dc in range(DC):
                for n in range(2):
                    last = (dc == DC - 1 and n == 1)
                    E.do('tensor', lambda e, d=dc, nn=n, p=p2, ps=pP: e.matmul(
                        H[ps][:, nn * 512:(nn + 1) * 512],
                        H['wg%d' % p][:, d * 128:(d + 1) * 128],
                        H['actp'][:, d * TMY + nn * 512: d * TMY + (nn + 1) * 512],
                        start=(d == 0), stop=(d == DC - 1)),
                        incs=([('sPE', 1)] if last else []),
                        waits=(waits if (dc == 0 and n == 0) else []))
            pe_after_w2[gi] = E.c['sPE']
            ngi = gi + 2
            if ngi < NE * DC:
                issue_wg(ngi // DC, ngi % DC, waits=[('sPE', pe_after_w2[gi])])
            E.do('vector', lambda e, gg=g, ps=pP: e.tensor_tensor(
                H['delta'][:, gg * TMY:(gg + 1) * TMY],
                H['delta'][:, gg * TMY:(gg + 1) * TMY], H[ps].ap(), AO.add),
                incs=[('sDVE', 1)],
                waits=[('sPE', pe_after_w2[gi]), ('sDVE', E.c['sDVE'])])
            dve_after_dacc[gi] = E.c['sDVE']
            if g % 2 == 1:
                pair_last[0] = E.c['sDVE']

    dve_p5_done = E.c['sDVE']
    pe_p5_done = E.c['sPE']

    for f in range(DC):
        E.do('sync', lambda e, ff=f: e.dma_start(
            out=H['xn_rl_%d' % ff].ap(),
            in_=H['xn_spill'][:, ff * TMY:(ff + 1) * TMY]),
            incs=[('sXN', 16)],
            waits=[('sSPL', 16), ('sPE', pe_p5_done)])
    n_xn = E.c['sXN']

    dve_after_x2 = [0] * DC
    for f in range(DC):
        E.do('vector', lambda e, ff=f: e.scalar_tensor_tensor(
            H['delta'][:, ff * TMY:(ff + 1) * TMY],
            H['xn_rl_%d' % ff].ap(), 2.0,
            H['delta'][:, ff * TMY:(ff + 1) * TMY], AO.mult, AO.add),
            incs=[('sDVE', 1)],
            waits=[('sXN', n_xn), ('sDVE', dve_p5_done), ('sDVE', E.c['sDVE'])])
        dve_after_x2[f] = E.c['sDVE']

    for f in range(DC):
        E.do('scalar', lambda e, ff=f: e.activation(
            H['sqb'].ap(), H['delta'][:, ff * TMY:(ff + 1) * TMY], AF.Square),
            incs=[('sACT', 1)],
            waits=[('sDVE', dve_after_x2[f]), ('sPE', E.c['sPE'])])
        act_n = E.c['sACT']
        for n in range(2):
            E.do('tensor', lambda e, ff=f, nn=n: e.matmul(
                H['pD'][0:1, nn * 512:(nn + 1) * 512], H['ones_f32'][:, 0:1],
                H['delta'][:, ff * TMY + nn * 512: ff * TMY + (nn + 1) * 512],
                start=(ff == 0), stop=(ff == DC - 1)),
                incs=[], waits=[('sDVE', dve_after_x2[f])])
        for n in range(2):
            last = (n == 1)
            E.do('tensor', lambda e, ff=f, nn=n: e.matmul(
                H['pC'][0:1, nn * 512:(nn + 1) * 512], H['ones_f32'][:, 0:1],
                H['sqb'][:, nn * 512:(nn + 1) * 512],
                start=(ff == 0), stop=(ff == DC - 1)),
                incs=([('sPE', 1)] if last else []),
                waits=[('sACT', act_n)])
    pe_stats3 = E.c['sPE']

    dve_ln2 = ln_finalize([('sPE', pe_stats3)])

    dve_after_out = [0] * DC
    for f in range(DC):
        E.do('vector', lambda e, ff=f: e.tensor_tensor(
            H['delta'][:, ff * TMY:(ff + 1) * TMY],
            H['delta'][:, ff * TMY:(ff + 1) * TMY], H['bcA'].ap(), AO.mult),
            incs=[('sDVE', 1)],
            waits=[('sDVE', dve_ln2), ('sDVE', E.c['sDVE'])])
        E.do('vector', lambda e, ff=f: e.tensor_tensor(
            H['delta'][:, ff * TMY:(ff + 1) * TMY],
            H['delta'][:, ff * TMY:(ff + 1) * TMY], H['bcB'].ap(), AO.add),
            incs=[('sDVE', 1)], waits=[('sDVE', E.c['sDVE'])])
        dve_after_out[f] = E.c['sDVE']

    sty = 0
    dve_otcopy = {0: 0, 1: 0}
    for tk in range(8):
        for half in range(2):
            pk = ['pC', 'pD'][half]
            waits = [('sDVE', dve_after_out[DC - 1]),
                     ('sDVE', dve_otcopy[half])]
            for i in range(8):
                f = half * 8 + i
                last = (i == 7)
                E.do('tensor', lambda e, ff=f, tt=tk, ii=i, pkk=pk: e.transpose(
                    H[pkk][:, ii * 128:(ii + 1) * 128],
                    H['delta'][:, ff * TMY + tt * 128: ff * TMY + (tt + 1) * 128],
                    H['eye_f32'].ap()),
                    incs=([('sPE', 1)] if last else []),
                    waits=(waits if i == 0 else []))
            pen = E.c['sPE']
            E.do('vector', lambda e, hf=half, pkk=pk: e.tensor_copy(
                H['outtok'][:, hf * 1024:(hf + 1) * 1024], H[pkk].ap()),
                incs=[('sDVE', 1)],
                waits=[('sPE', pen), ('sDVE', E.c['sDVE']),
                       ('sSTy', 16 * sty)])
            dve_otcopy[half] = E.c['sDVE']
        dven = E.c['sDVE']
        E.do('sync', lambda e, tt=tk: e.dma_start(
            out=H['o_y'][tt * 128:(tt + 1) * 128, :], in_=H['outtok'].ap()),
            incs=[('sSTy', 16)], waits=[('sDVE', dven)])
        sty += 1

    if sel == 'sync':
        E.eng.wait_ge(H['sSTy'], 16 * sty)


def _build_nc():
    nc = bass.Bass('TRN2', target_bir_lowering=False)
    H = {}

    def din(name, shape, dtype=F32):
        H[name] = nc.dram_tensor(name, list(shape), dtype, kind='ExternalInput')

    din('i_x', (T, D))
    din('i_xmy', (TMY, D))
    din('i_wqkv', (128, 16 * 6 * 128))
    din('i_bqkv', (128, 6))
    din('i_wout', (16, 128, 16 * 128))
    din('i_bout', (128, 16))
    din('i_nw', (128, 16))
    din('i_gwT', (128, 256))
    din('i_gb', (1, 16))
    din('i_w1a', (NE, DC, 128, 16 * 128))
    din('i_w1b', (NE, DC, 128, 16 * 128))
    din('i_b1a', (128, 256))
    din('i_b1b', (128, 256))
    din('i_w2', (NE, DC, 128, 16 * 128))
    din('i_b2', (16, D))
    din('i_oneh', (16, 16 * 128))
    din('c_eye_f32', (128, 128))
    din('c_ones_f32', (128, 128))

    H['o_y'] = nc.dram_tensor('o_y', [TMY, D], F32, kind='ExternalOutput')

    H['cc_in'] = nc.dram_tensor('cc_in', [256, T], BF, kind='Internal')
    H['cc_out0'] = nc.dram_tensor('cc_out0', [NCORES * 128, T], BF, kind='Internal')
    H['cc_out1'] = nc.dram_tensor('cc_out1', [NCORES * 128, T], BF, kind='Internal')
    H['xn_spill'] = nc.dram_tensor('xn_spill', [128, DC * TMY], F32, kind='Internal')

    psize = nc.SBUF_PARTITION_SIZE_BYTES
    LAYOUT_BYTES = 210560
    region = nc.alloc_sbuf_tensor('layout_region', [128, LAYOUT_BYTES],
                                  mybir.dt.uint8)
    BASE = nc.lookup_mloc(region).addr
    assert BASE + LAYOUT_BYTES <= psize - 2 * KB, (BASE, psize)

    def sb_at(name, shape, dtype, offset):
        offset += BASE
        nbytes = int(np.prod(shape[1:])) * mybir.dt.size(dtype)
        assert offset % 32 == 0, (name, offset)
        assert offset + nbytes <= BASE + LAYOUT_BYTES, (name, offset, nbytes)
        H[name] = nc.alloc_sbuf_tensor_at(name, list(shape), dtype, offset=offset)

    cur = 0
    s_off = {}

    def sbs(name, shape, dtype):
        nonlocal cur
        nbytes = int(np.prod(shape[1:])) * mybir.dt.size(dtype)
        cur = (cur + 31) & ~31
        s_off[name] = cur
        sb_at(name, shape, dtype, cur)
        cur += nbytes

    sbs('eye_bf', (128, 128), BF)
    sbs('eye_f32', (128, 128), F32)
    sbs('ones_f32', (128, 128), F32)
    sbs('ones_bf', (128, 128), BF)
    sbs('bqkv_c', (128, 6), F32)
    sbs('bout_c', (128, 16), F32)
    sbs('nw_c', (128, 16), F32)
    sbs('b1a_c', (128, 256), F32)
    sbs('b1b_c', (128, 256), F32)
    sbs('gwT', (128, 256), BF)
    sbs('gb', (1, 16), F32)
    sbs('G_sb', (128, 128), F32)
    sbs('bcC', (128, 1024), F32)
    sbs('sqb', (128, 1024), F32)
    sbs('E0', (128, 1024), BF)
    sbs('E1', (128, 1024), BF)
    sbs('att0', (128, 1024), BF)
    sbs('att1', (128, 1024), BF)
    sbs('vstage', (128, 256), BF)
    sbs('cbias', (128, 2), F32)
    sbs('minis', (1, 16), F32)
    sbs('Ltmp', (128, 16), F32)
    sbs('eqm', (128, 16), F32)
    sbs('expv', (128, 16), F32)
    sbs('mq4', (128, 8), F32)
    sbs('actp', (128, DC * 1024), BF)

    sb_at('GT_sb', (16, 1024), F32, s_off['bcC'])
    sb_at('Gbc', (128, 1024), F32, s_off['sqb'])
    sb_at('sacc_d', (128, 1024), F32, s_off['actp'])
    sb_at('sacc_p', (128, 1024), F32, s_off['actp'] + 4 * KB)
    cur = (cur + 31) & ~31
    ew_off = cur
    sb_at('ew_a', (128, 1024), F32, ew_off)
    sb_at('ew_s', (128, 1024), F32, ew_off + 4 * KB)
    sb_at('outtok', (128, 2048), F32, ew_off)
    sb_at('rbc', (128, 1024), F32, ew_off)
    cur = ew_off + 8 * KB

    cur = (cur + 31) & ~31
    bc_off = cur
    sb_at('bcA', (128, 1024), F32, bc_off)
    sb_at('bcB', (128, 1024), F32, bc_off + 4 * KB)
    sb_at('oneh', (16, 16 * 128), F32, bc_off)
    cur = bc_off + 8 * KB

    S_END = (cur + 63) & ~63

    A = S_END
    sb_at('qA', (128, T), BF, A)
    sb_at('qB', (128, T), BF, A + 16 * KB)
    sb_at('kA', (128, T), BF, A + 32 * KB)
    sb_at('kB', (128, T), BF, A + 48 * KB)
    sb_at('x_fm', (128, DC * TMY), F32, A)
    sb_at('delta', (128, DC * TMY), F32, A)

    B = A + 64 * KB
    sb_at('vTA', (128, T), BF, B)
    sb_at('vTB', (128, T), BF, B + 16 * KB)
    sb_at('attn_all', (128, DC * TMY), BF, B)
    sb_at('t_sb', (128, DC * TMY), BF, B)

    C = B + 32 * KB
    sb_at('wqkv_sb', (128, 16 * 6 * 128), BF, C)
    for i in range(4):
        sb_at('wo%d' % i, (128, 2048), BF, C + i * 4 * KB)
    sb_at('wa0', (128, 2048), BF, C)
    sb_at('wa1', (128, 2048), BF, C + 4 * KB)
    sb_at('wb0', (128, 2048), BF, C + 8 * KB)
    sb_at('wb1', (128, 2048), BF, C + 12 * KB)
    sb_at('wg0', (128, 2048), BF, C + 16 * KB)
    sb_at('wg1', (128, 2048), BF, C + 20 * KB)
    sb_at('r0', (1, 1024), F32, C + 16 * KB)
    sb_at('r1', (1, 1024), F32, C + 20 * KB)

    Dz = C + 24 * KB
    sb_at('xtok0', (128, 2048), BF, Dz)
    sb_at('xtok1', (128, 2048), BF, Dz + 4 * KB)
    sb_at('xT', (128, 2048), BF, Dz + 8 * KB)
    sb_at('xrel0', (128, 2048), F32, Dz)
    sb_at('xrel1', (128, 2048), F32, Dz + 8 * KB)
    sb_at('b2_sb', (16, D), F32, Dz)
    sb_at('r2', (1, 1024), F32, Dz + 8 * KB)
    sb_at('r3', (1, 1024), F32, Dz + 12 * KB)

    D_END = Dz + 16 * KB
    assert D_END <= LAYOUT_BYTES, (LAYOUT_BYTES, D_END)
    xn_spots = [B + i * 4 * KB for i in range(8)]
    xn_spots += [C + i * 4 * KB for i in range(4)]
    xn_spots += [Dz, Dz + 4 * KB]
    e_off = (s_off['E0'] + 31) & ~31
    assert s_off['att1'] + 2 * KB - e_off >= 8 * KB - 64, (s_off,)
    xn_spots += [e_off, e_off + 4 * KB]
    for i, o in enumerate(xn_spots):
        sb_at('xn_rl_%d' % i, (128, 1024), F32, o)

    for p in ('pA', 'pB', 'pC', 'pD'):
        H[p] = nc.alloc_psum_tensor(p, [128, 1024], F32)

    for s in SEMS:
        H[s] = nc.alloc_semaphore(s)

    return nc, H


def build():
    nc, H = _build_nc()
    with nc.Block() as block:
        def mk(sel):
            def f(eng):
                _emit_program(Emitter(nc, H, sel, eng))
            return f
        block.tensor(mk('tensor'))
        block.scalar(mk('scalar'))
        block.vector(mk('vector'))
        block.sync(mk('sync'))
        block.gpsimd(mk('gpsimd'))
    return nc


_NC_CACHE = None


def _get_nc():
    global _NC_CACHE
    if _NC_CACHE is None:
        _NC_CACHE = build()
    return _NC_CACHE


def _host_prep(inputs):
    x = np.asarray(inputs['L0_attn_x'], dtype=np.float32).reshape(T, D)
    w_qkv = np.asarray(inputs['L0_attn_w_qkv'], dtype=np.float32)
    b_qkv = np.asarray(inputs['L0_attn_b_qkv'], dtype=np.float32)
    w_out = np.asarray(inputs['L0_attn_w_out'], dtype=np.float32)
    b_out = np.asarray(inputs['L0_attn_b_out'], dtype=np.float32)
    norm_w = np.asarray(inputs['L0_ffn_norm_w'], dtype=np.float32)
    gate_w = np.asarray(inputs['L0_ffn_gate_w'], dtype=np.float32)
    gate_b = np.asarray(inputs['L0_ffn_gate_b'], dtype=np.float32)
    w1 = np.asarray(inputs['L0_ffn_w1'], dtype=np.float32)
    b1 = np.asarray(inputs['L0_ffn_b1'], dtype=np.float32)
    w2 = np.asarray(inputs['L0_ffn_w2'], dtype=np.float32)
    b2 = np.asarray(inputs['L0_ffn_b2'], dtype=np.float32)

    shared = {}
    shared['i_x'] = np.ascontiguousarray(x)
    shared['i_wout'] = np.ascontiguousarray(
        w_out.reshape(16, 128, 16, 128).transpose(2, 1, 0, 3).reshape(16, 128, 2048))
    shared['i_bout'] = np.ascontiguousarray(b_out.reshape(16, 128).T)
    shared['i_nw'] = np.ascontiguousarray(norm_w.reshape(16, 128).T)
    shared['i_gwT'] = np.ascontiguousarray(
        gate_w.T.reshape(16, 128, 16).transpose(1, 0, 2).reshape(128, 256))
    shared['i_gb'] = np.ascontiguousarray(gate_b.reshape(1, 16))

    def blk_w1(wh):
        whT = wh.transpose(0, 2, 1)
        return np.ascontiguousarray(
            whT.reshape(NE, 16, 128, 16, 128).transpose(0, 3, 2, 1, 4)
               .reshape(NE, 16, 128, 2048))

    shared['i_w1a'] = blk_w1(w1[:, 0::2, :])
    shared['i_w1b'] = blk_w1(w1[:, 1::2, :])
    w2T = w2.transpose(0, 2, 1)
    shared['i_w2'] = np.ascontiguousarray(
        w2T.reshape(NE, 16, 128, 16, 128).transpose(0, 3, 2, 1, 4)
           .reshape(NE, 16, 128, 2048))

    def col16(v):
        return np.ascontiguousarray(
            v.reshape(16, 16, 128).transpose(2, 0, 1).reshape(128, 256))

    shared['i_b1a'] = col16(b1[:, 0::2])
    shared['i_b1b'] = col16(b1[:, 1::2])
    shared['i_b2'] = np.ascontiguousarray(b2)

    oneh = np.zeros((16, 16 * 128), dtype=np.float32)
    for e in range(16):
        oneh[e, e * 128:(e + 1) * 128] = 1.0
    shared['i_oneh'] = oneh
    shared['c_eye_f32'] = np.eye(128, dtype=np.float32)
    shared['c_ones_f32'] = np.ones((128, 128), dtype=np.float32)

    in_maps = []
    for i in range(NCORES):
        m = dict(shared)
        m['i_xmy'] = np.ascontiguousarray(x[i * TMY:(i + 1) * TMY, :])
        cols = np.concatenate([np.arange(256 * i, 256 * (i + 1)),
                               2048 + np.arange(256 * i, 256 * (i + 1)),
                               4096 + np.arange(256 * i, 256 * (i + 1))])
        wq = w_qkv[:, cols]
        m['i_wqkv'] = np.ascontiguousarray(
            wq.reshape(16, 128, 6, 128).transpose(1, 0, 2, 3).reshape(128, 16 * 6 * 128))
        bq = b_qkv[cols]
        m['i_bqkv'] = np.ascontiguousarray(bq.reshape(6, 128).T)
        in_maps.append(m)
    return in_maps


def kernel(**inputs):
    nc = _get_nc()
    in_maps = _host_prep(inputs)
    res = run_bass_kernel_spmd(nc, in_maps, core_ids=list(range(NCORES)))
    y = np.concatenate([np.asarray(r['o_y']) for r in res.results], axis=0)
    return y.reshape(1, T, D).astype(np.float32)


# version 04
# speedup vs baseline: 1.1447x; 1.1447x over previous
import numpy as np
import ml_dtypes
import concourse.bass as bass
import concourse.mybir as mybir
from concourse.bass_utils import run_bass_kernel_spmd

F32 = mybir.dt.float32
BF = mybir.dt.bfloat16
AO = mybir.AluOpType
AF = mybir.ActivationFunctionType
AX = mybir.AxisListType

NCORES = 8
D = 2048
T = 8192
TMY = T // NCORES
DC = D // 128
NE = 16
S_ATT = float(1.0 / np.sqrt(128.0))
ALPHA = 1.702
LIMIT = 7.0
EPS = 1e-5
BIG = 1.0e30

KB = 1024

SEMS = [
    'sPE', 'sACT', 'sDVE',
    'sLDc',
    'sGLD',
    'sLDx0', 'sLDx1',
    'sSTc0', 'sSTc1',
    'sCC1',
    'sATT',
    'sXR0', 'sXR1',
    'sWo0', 'sWo1', 'sWo2', 'sWo3',
    'sSPL',
    'sXN',
    'sWa0', 'sWa1',
    'sWg0', 'sWg1',
    'sLD2',
    'sPOOL',
    'sCC2',
    'sATT2',
    'sSTy',
]


class Emitter:

    def __init__(self, nc, H, sel, eng):
        self.nc = nc
        self.H = H
        self.sel = sel
        self.eng = eng
        self.c = {s: 0 for s in SEMS}
        self._lastwait = {}

    def do(self, eng_name, fn, incs=(), waits=()):
        if self.sel == eng_name:
            for (s, v) in waits:
                if v <= 0:
                    continue
                if self._lastwait.get(s, -1) >= v:
                    continue
                self._lastwait[s] = v
                self.eng.wait_ge(self.H[s], v)
            ins = fn(self.eng)
            for (s, n) in incs:
                ins.then_inc(self.H[s], n)
        for (s, n) in incs:
            self.c[s] += n


def _emit_program(E):
    nc = E.nc
    H = E.H
    sel = E.sel

    sp_loads = [
        ('eye_f32', 'c_eye_f32'), ('ones_f32', 'c_ones_f32'),
        ('bqkv_c', 'i_bqkv'), ('bout_c', 'i_bout'), ('nw_c', 'i_nw'),
        ('b1a_c', 'i_b1a'), ('b1b_c', 'i_b1b'), ('gb', 'i_gb'),
    ]
    for dst, src in sp_loads:
        E.do('sync', lambda e, d=dst, s=src: e.dma_start(
            out=H[d].ap(), in_=H[s].ap()), incs=[('sLDc', 16)])
    n_sp_loads = E.c['sLDc']

    gp_loads = [
        ('eye_bf', 'c_eye_f32'), ('ones_bf', 'c_ones_f32'), ('gwT', 'i_gwT'),
        ('wqkv_sb', 'i_wqkv'),
    ]
    for dst, src in gp_loads:
        E.do('gpsimd', lambda e, d=dst, s=src: e.dma_start(
            out=H[d].ap(), in_=H[s].ap()), incs=[('sGLD', 16)])
    n_gp_loads = E.c['sGLD']

    E.do('vector', lambda e: e.memset(H['minis'].ap(), 0.0), incs=[('sDVE', 1)])
    E.do('vector', lambda e: e.memset(H['minis'][0:1, 0:1], EPS),
         incs=[('sDVE', 1)], waits=[('sDVE', E.c['sDVE'])])
    mini_ready = E.c['sDVE']

    NBLK = 64
    TB = 128
    pe_after_qkv = [0] * NBLK
    pe_after_pack = [0] * NBLK
    pe_after_vtr = [0] * NBLK
    dve_after_xcopy = [0] * NBLK
    dve_after_evac = [0] * NBLK
    dve_after_vcopy = [0] * NBLK

    for bi in range(NBLK):
        par = bi % 2
        pk = ['pC', 'pD'][par]
        sem_x = 'sLDx%d' % par
        waits = []
        if bi >= 2:
            waits.append(('sPE', pe_after_pack[bi - 2]))
        E.do('gpsimd', lambda e, b=bi, p=par: e.dma_start(
            out=H['xtok%d' % p].ap(),
            in_=H['i_x'][b * TB:(b + 1) * TB, :]),
            incs=[(sem_x, 16)], waits=waits)
        n_loads_par = (bi // 2) + 1

        waits = [(sem_x, 16 * n_loads_par)]
        if bi >= 2:
            waits.append(('sDVE', dve_after_vcopy[bi - 2]))
        if bi == 0:
            waits.append(('sGLD', n_gp_loads))
        for dc in range(DC):
            last = (dc == DC - 1)
            E.do('tensor', lambda e, d=dc, p=par, pkk=pk: e.transpose(
                H[pkk].ap().bitcast(BF)[:, d * 128:(d + 1) * 128],
                H['xtok%d' % p][:, d * 128:(d + 1) * 128],
                H['eye_bf'].ap()),
                incs=([('sPE', 1)] if last else []),
                waits=(waits if dc == 0 else []))
        pe_after_pack[bi] = E.c['sPE']

        waits = [('sPE', pe_after_pack[bi]), ('sDVE', E.c['sDVE'])]
        if bi >= 1:
            waits.append(('sPE', pe_after_qkv[bi - 1]))
        E.do('vector', lambda e, pkk=pk: e.tensor_copy(
            H['xT'].ap(), H[pkk].ap().bitcast(BF)),
            incs=[('sDVE', 1)], waits=waits)
        dve_after_xcopy[bi] = E.c['sDVE']

        waits = [('sDVE', dve_after_xcopy[bi])]
        if bi == 0:
            waits.append(('sGLD', n_gp_loads))
        if bi >= 1:
            waits.append(('sDVE', dve_after_evac[bi - 1]))
        first = True
        for j in range(6):
            for dc in range(DC):
                last = (j == 5 and dc == DC - 1)
                E.do('tensor', lambda e, jj=j, d=dc: e.matmul(
                    H['pA'][:, jj * TB:(jj + 1) * TB],
                    H['wqkv_sb'][:, (d * 6 + jj) * 128:(d * 6 + jj + 1) * 128],
                    H['xT'][:, d * TB:(d + 1) * TB],
                    start=(d == 0), stop=(d == DC - 1)),
                    incs=([('sPE', 1)] if last else []),
                    waits=(waits if first else []))
                first = False
        pe_after_qkv[bi] = E.c['sPE']

        waits = [('sPE', pe_after_qkv[bi]), ('sLDc', n_sp_loads),
                 ('sDVE', E.c['sDVE'])]
        dests = ['qA', 'qB', 'kA', 'kB']
        for j in range(6):
            last = (j == 5)

            def evac(e, jj=j, b=bi):
                if jj < 4:
                    out_ap = H[dests[jj]][:, b * TB:(b + 1) * TB]
                else:
                    out_ap = H['vstage'][:, (jj - 4) * TB:(jj - 3) * TB]
                return e.tensor_scalar(out_ap, H['pA'][:, jj * TB:(jj + 1) * TB],
                                       H['bqkv_c'][:, jj:jj + 1], None, AO.add)
            E.do('vector', evac,
                 incs=([('sDVE', 1)] if last else []),
                 waits=(waits if j == 0 else []))
        dve_after_evac[bi] = E.c['sDVE']

        waits = [('sDVE', dve_after_evac[bi]), ('sDVE', dve_after_xcopy[bi])]
        for h in range(2):
            last = (h == 1)
            E.do('tensor', lambda e, hh=h, pkk=pk: e.transpose(
                H[pkk].ap().bitcast(BF)[:, hh * 128:(hh + 1) * 128],
                H['vstage'][:, hh * 128:(hh + 1) * 128],
                H['eye_bf'].ap()),
                incs=([('sPE', 1)] if last else []),
                waits=(waits if h == 0 else []))
        pe_after_vtr[bi] = E.c['sPE']
        for h in range(2):
            E.do('vector', lambda e, hh=h, b=bi, pkk=pk: e.tensor_copy(
                H['vT%s' % 'AB'[hh]][:, b * TB:(b + 1) * TB],
                H[pkk].ap().bitcast(BF)[:, hh * 128:(hh + 1) * 128]),
                incs=([('sDVE', 1)] if h == 1 else []),
                waits=([('sPE', pe_after_vtr[bi]), ('sDVE', E.c['sDVE'])]
                       if h == 0 else []))
        dve_after_vcopy[bi] = E.c['sDVE']

    pe_p0_done = E.c['sPE']
    dve_p0_done = E.c['sDVE']

    wo_issued = [0, 0, 0, 0]

    def issue_wo(f, extra_waits=()):
        p4 = f % 4
        E.do('gpsimd', lambda e, ff=f, p=p4: e.dma_start(
            out=H['wo%d' % p].ap(), in_=H['i_wout'][ff, :, :]),
            incs=[('sWo%d' % p4, 16)], waits=list(extra_waits))
        wo_issued[p4] += 1

    for f in range(4):
        issue_wo(f, extra_waits=[('sPE', pe_p0_done)])

    NQB = T // 1024
    NKC = T // 128
    dve_after_attst = {}
    stc_count = [0, 0]
    last_exp_par = {0: 0, 1: 0}
    last_av_par = {0: 0, 1: 0}
    last_attst = 0
    sq_war = {0: 0, 1: 0}
    last_sig_dve = [0]
    last_sig_pool = [0]
    pe_sig_final = [0]

    for h in range(2):
        qn = 'q%s' % 'AB'[h]
        kn = 'k%s' % 'AB'[h]
        vn = 'vT%s' % 'AB'[h]

        for (src, mslot) in ((qn, 2), (kn, 3)):
            E.do('vector', lambda e, ms=mslot: e.memset(H['minis'][0:1, ms:ms + 1], 0.0),
                 incs=[('sDVE', 1)], waits=[('sDVE', E.c['sDVE'])])
            for j in range(16):
                p2 = j % 2
                E.do('scalar', lambda e, jj=j, p=p2, s=src: e.activation(
                    H['sqb'][:, p * 512:(p + 1) * 512],
                    H[s][:, jj * 512:(jj + 1) * 512], AF.Square),
                    incs=[('sACT', 1)],
                    waits=[('sDVE', dve_p0_done), ('sPE', sq_war[p2])])
                act_n = E.c['sACT']
                E.do('tensor', lambda e, p=p2: e.matmul(
                    H['pD'][0:1, p * 512:(p + 1) * 512],
                    H['ones_f32'][:, 0:1], H['sqb'][:, p * 512:(p + 1) * 512],
                    start=True, stop=True),
                    incs=[('sPE', 1)],
                    waits=[('sACT', act_n), ('sDVE', E.c['sDVE'])])
                sq_war[p2] = E.c['sPE']
                pe_n = E.c['sPE']
                E.do('vector', lambda e, p=p2: e.reduce_max(
                    H['minis'][0:1, 4:5], H['pD'][0:1, p * 512:(p + 1) * 512],
                    axis=AX.X),
                    incs=[('sDVE', 1)],
                    waits=[('sPE', pe_n), ('sDVE', E.c['sDVE'])])
                E.do('vector', lambda e, ms=mslot: e.tensor_tensor(
                    H['minis'][0:1, ms:ms + 1], H['minis'][0:1, ms:ms + 1],
                    H['minis'][0:1, 4:5], AO.max),
                    incs=[('sDVE', 1)], waits=[('sDVE', E.c['sDVE'])])
        E.do('scalar', lambda e: e.activation(
            H['minis'][0:1, 5:6], H['minis'][0:1, 2:3], AF.Sqrt),
            incs=[('sACT', 1)], waits=[('sDVE', E.c['sDVE'])])
        E.do('scalar', lambda e: e.activation(
            H['minis'][0:1, 6:7], H['minis'][0:1, 3:4], AF.Sqrt),
            incs=[('sACT', 1)], waits=[])
        E.do('vector', lambda e: e.tensor_tensor(
            H['minis'][0:1, 7:8], H['minis'][0:1, 5:6], H['minis'][0:1, 6:7],
            AO.mult),
            incs=[('sDVE', 1)],
            waits=[('sACT', E.c['sACT']), ('sDVE', E.c['sDVE'])])
        E.do('vector', lambda e: e.tensor_scalar(
            H['minis'][0:1, 8:9], H['minis'][0:1, 7:8], -S_ATT, None, AO.mult),
            incs=[('sDVE', 1)], waits=[('sDVE', E.c['sDVE'])])
        dven = E.c['sDVE']
        E.do('tensor', lambda e: e.matmul(
            H['pD'][:, 0:1], H['ones_f32'][0:1, :], H['minis'][0:1, 8:9],
            start=True, stop=True),
            incs=[('sPE', 1)], waits=[('sDVE', dven)])
        pen = E.c['sPE']
        E.do('vector', lambda e, hh=h: e.tensor_copy(
            H['cbias'][:, hh:hh + 1], H['pD'][:, 0:1]),
            incs=[('sDVE', 1)], waits=[('sPE', pen), ('sDVE', E.c['sDVE'])])
        dve_cbias = E.c['sDVE']

        for qb in range(NQB):
            act_exp_q = [0] * NKC

            def emit_scores(kc):
                spar = kc % 2
                pS = ['pA', 'pB'][spar]
                waits = [('sACT', last_exp_par[spar])]
                if qb == 0 and kc < 2:
                    waits.append(('sDVE', dve_cbias))
                for n in range(2):
                    E.do('tensor', lambda e, k=kc, nn=n, q=qb, ps=pS: e.matmul(
                        H[ps][:, nn * 512:(nn + 1) * 512],
                        H[kn][:, k * 128:(k + 1) * 128],
                        H[qn][:, q * 1024 + nn * 512: q * 1024 + (nn + 1) * 512],
                        start=True, stop=True),
                        incs=([('sPE', 1)] if n == 1 else []),
                        waits=(waits if n == 0 else []))
                pe_score = E.c['sPE']
                E.do('scalar', lambda e, k=kc, ps=pS, hh=h: e.activation(
                    H['E%d' % (k % 2)].ap(), H[ps].ap(),
                    AF.Exp, bias=H['cbias'][:, hh:hh + 1], scale=S_ATT),
                    incs=[('sACT', 1)],
                    waits=[('sPE', pe_score), ('sPE', last_av_par[kc % 2]),
                           ('sDVE', last_sig_dve[0]),
                           ('sPOOL', last_sig_pool[0])])
                last_exp_par[spar] = E.c['sACT']
                act_exp_q[kc] = E.c['sACT']

            def emit_av(kc):
                waits = [('sACT', act_exp_q[kc])]
                if kc == 0:
                    waits.append(('sDVE', last_attst))
                for n in range(2):
                    last = (n == 1)
                    E.do('tensor', lambda e, k=kc, nn=n: e.matmul(
                        H['pC'][:, nn * 512:(nn + 1) * 512],
                        H[vn][:, k * 128:(k + 1) * 128],
                        H['E%d' % (k % 2)][:, nn * 512:(nn + 1) * 512],
                        start=(k == 0), stop=(k == NKC - 1)),
                        incs=([('sPE', 1)] if last else []),
                        waits=(waits if n == 0 else []))
                last_av_par[kc % 2] = E.c['sPE']

            def emit_sig(kc):
                for (eng, sem, acc, lo) in (
                        ('vector', 'sDVE', 'sacc_d', 0),
                        ('gpsimd', 'sPOOL', 'sacc_p', 512)):
                    waits = [('sACT', act_exp_q[kc]), (sem, E.c[sem])]
                    if kc == 0:
                        waits.append(('sPE', pe_sig_final[0]))
                    E.do(eng, lambda e, k=kc, a=acc, o=lo: (
                        e.tensor_copy(H[a][:, 0:512],
                                      H['E%d' % (k % 2)][:, o:o + 512])
                        if k == 0 else
                        e.tensor_tensor(H[a][:, 0:512], H[a][:, 0:512],
                                        H['E%d' % (k % 2)][:, o:o + 512],
                                        AO.add)),
                        incs=[(sem, 1)], waits=waits)
                    if sem == 'sDVE':
                        last_sig_dve[0] = E.c[sem]
                    else:
                        last_sig_pool[0] = E.c[sem]

            emit_scores(0)
            for kc in range(1, NKC):
                emit_scores(kc)
                emit_av(kc - 1)
                emit_sig(kc - 1)
            emit_av(NKC - 1)
            emit_sig(NKC - 1)
            pe_av_last = E.c['sPE']

            waits = [('sDVE', last_sig_dve[0]), ('sPOOL', last_sig_pool[0]),
                     ('sDVE', last_attst)]
            for k2, (acc, lo) in enumerate((('sacc_d', 0), ('sacc_p', 512))):
                E.do('tensor', lambda e, a=acc, o=lo: e.matmul(
                    H['pD'][0:1, o:o + 512],
                    H['ones_f32'][:, 0:1], H[a][:, 0:512],
                    start=True, stop=True),
                    incs=([('sPE', 1)] if k2 == 1 else []),
                    waits=(waits if k2 == 0 else []))
            pe_sig_final[0] = E.c['sPE']

            E.do('vector', lambda e: e.tensor_copy(H['r0'].ap(), H['pD'][0:1, 0:1024]),
                 incs=[('sDVE', 1)],
                 waits=[('sPE', pe_sig_final[0]), ('sPE', pe_av_last),
                        ('sDVE', E.c['sDVE'])])
            E.do('vector', lambda e: e.reciprocal(H['r1'].ap(), H['r0'].ap()),
                 incs=[('sDVE', 1)], waits=[('sDVE', E.c['sDVE'])])
            dven = E.c['sDVE']
            for n in range(2):
                E.do('tensor', lambda e, nn=n: e.matmul(
                    H['pD'][:, nn * 512:(nn + 1) * 512],
                    H['ones_f32'][0:1, :], H['r1'][0:1, nn * 512:(nn + 1) * 512],
                    start=True, stop=True),
                    incs=([('sPE', 1)] if n == 1 else []),
                    waits=([('sDVE', dven)] if n == 0 else []))
            pen = E.c['sPE']
            E.do('vector', lambda e: e.tensor_copy(H['rbc'].ap(), H['pD'].ap()),
                 incs=[('sDVE', 1)], waits=[('sPE', pen), ('sDVE', E.c['sDVE'])])
            apar = (h * NQB + qb) % 2
            E.do('vector', lambda e, ap_=apar: e.tensor_tensor(
                H['att%d' % ap_].ap(), H['pC'].ap(), H['rbc'].ap(), AO.mult),
                incs=[('sDVE', 1)],
                waits=[('sDVE', E.c['sDVE']),
                       ('sSTc%d' % apar, 16 * stc_count[apar])])
            dve_after_attst[(h, qb)] = E.c['sDVE']
            last_attst = E.c['sDVE']
            E.do('sync', lambda e, hh=h, q=qb, ap_=apar: e.dma_start(
                out=H['cc_in'][hh * 128:(hh + 1) * 128, q * 1024:(q + 1) * 1024],
                in_=H['att%d' % ap_].ap()),
                incs=[('sSTc%d' % apar, 16)],
                waits=[('sDVE', dve_after_attst[(h, qb)])])
            stc_count[apar] += 1

        E.do('gpsimd', lambda e, hh=h: e.collective_compute(
            'AllGather', AO.bypass, replica_groups=[list(range(NCORES))],
            ins=[H['cc_in'][hh * 128:(hh + 1) * 128, :].opt()],
            outs=[H['cc_out%d' % hh].ap().opt()]),
            incs=[('sCC%d' % (h + 1), 1)],
            waits=[('sSTc0', 16 * stc_count[0]), ('sSTc1', 16 * stc_count[1])])

    pe_p1_done = E.c['sPE']

    if sel == 'sync':
        pid = nc.partition_id()
        att_off = pid * TMY
    for hh in range(2):
        sematt = ['sATT', 'sATT2'][hh]
        for r in range(NCORES):
            pos = hh * 8 + r

            def ld(e, rr=r, p=pos, hx=hh):
                sym = bass.AP(H['cc_out%d' % hx], rr * 128 * T + att_off,
                              [[T, 128], [1, TMY]])
                return e.dma_start(
                    out=H['attn_all'][:, p * TMY:(p + 1) * TMY], in_=sym)
            E.do('sync', ld, incs=[(sematt, 16)],
                 waits=[('sCC%d' % (hh + 1), 1), ('sPE', pe_p1_done)])
    n_att = E.c['sATT']
    n_att2 = E.c['sATT2']

    def attn_pos(dc):
        return (dc % 2) * 8 + dc // 2

    xr_issued = [0, 0]

    def issue_xr(r, waits=()):
        p2 = r % 2
        E.do('sync', lambda e, rr=r, p=p2: e.dma_start(
            out=H['xrel%d' % p].ap(), in_=H['i_xmy'][rr * 128:(rr + 1) * 128, :]),
            incs=[('sXR%d' % p2, 16)], waits=list(waits))
        xr_issued[p2] += 1

    issue_xr(0)
    issue_xr(1)

    pe_after_xtr = [0] * 8
    dve_after_xfm = [0] * 8
    for r in range(8):
        p2 = r % 2
        for half in range(2):
            pk = ['pC', 'pD'][half]
            waits = [('sXR%d' % p2, 16 * ((r // 2) + 1))]
            if r == 0 and half == 0:
                waits.append(('sPE', pe_p1_done))
            if r >= 1:
                waits.append(('sDVE', dve_after_xfm[r - 1]))
            for i in range(8):
                dc = half * 8 + i
                last = (i == 7)
                E.do('tensor', lambda e, d=dc, p=p2, pkk=pk, ii=i: e.transpose(
                    H[pkk][:, ii * 128:(ii + 1) * 128],
                    H['xrel%d' % p][:, d * 128:(d + 1) * 128],
                    H['eye_f32'].ap()),
                    incs=([('sPE', 1)] if last else []),
                    waits=(waits if i == 0 else []))
            pen = E.c['sPE']
            E.do('vector', lambda e, rr=r, hf=half, pkk=pk: e.tensor_copy(
                H['x_fm'].ap().rearrange('p (d m) -> p d m', d=DC)[
                    :, hf * 8:(hf + 1) * 8, rr * 128:(rr + 1) * 128],
                H[pkk].ap().rearrange('p (i m) -> p i m', i=8)),
                incs=[('sDVE', 1)], waits=[('sPE', pen), ('sDVE', E.c['sDVE'])])
        pe_after_xtr[r] = E.c['sPE']
        dve_after_xfm[r] = E.c['sDVE']
        if r + 2 < 8:
            issue_xr(r + 2, waits=[('sPE', pe_after_xtr[r])])

    pe_after_proj = [0] * DC
    dve_after_x1 = [0] * DC
    dc_order = [2 * r for r in range(8)] + [2 * r + 1 for r in range(8)]
    for f in range(DC):
        p4 = f % 4
        pP = ['pA', 'pB'][f % 2]
        waits = [('sWo%d' % p4, 16 * ((f // 4) + 1)), ('sATT', n_att),
                 ('sDVE', dve_after_xfm[7])]
        if f >= 2:
            waits.append(('sDVE', dve_after_x1[f - 2]))
        for di, dc in enumerate(dc_order):
            xw = [('sATT2', n_att2)] if di == 8 else []
            for n in range(2):
                last = (di == DC - 1 and n == 1)
                E.do('tensor', lambda e, d=dc, nn=n, ps=pP, p=p4: e.matmul(
                    H[ps][:, nn * 512:(nn + 1) * 512],
                    H['wo%d' % p][:, d * 128:(d + 1) * 128],
                    H['attn_all'][:, attn_pos(d) * TMY + nn * 512:
                                  attn_pos(d) * TMY + (nn + 1) * 512],
                    start=(di == 0), stop=(di == DC - 1)),
                    incs=([('sPE', 1)] if last else []),
                    waits=((waits if (di == 0 and n == 0) else []) +
                           (xw if n == 0 else [])))
        pe_after_proj[f] = E.c['sPE']
        if f + 4 < DC:
            issue_wo(f + 4, extra_waits=[('sPE', pe_after_proj[f])])
        E.do('vector', lambda e, ff=f, ps=pP: e.scalar_tensor_tensor(
            H['x_fm'][:, ff * TMY:(ff + 1) * TMY],
            H[ps].ap(), H['bout_c'][:, ff:ff + 1],
            H['x_fm'][:, ff * TMY:(ff + 1) * TMY],
            AO.add, AO.add),
            incs=[('sDVE', 1)],
            waits=[('sPE', pe_after_proj[f]), ('sDVE', E.c['sDVE'])])
        dve_after_x1[f] = E.c['sDVE']

    pe_stats1 = 0
    for f in range(DC):
        E.do('scalar', lambda e, ff=f: e.activation(
            H['sqb'].ap(), H['x_fm'][:, ff * TMY:(ff + 1) * TMY], AF.Square),
            incs=[('sACT', 1)],
            waits=[('sDVE', dve_after_x1[f]), ('sPE', E.c['sPE'])])
        act_n = E.c['sACT']
        for n in range(2):
            E.do('tensor', lambda e, ff=f, nn=n: e.matmul(
                H['pD'][0:1, nn * 512:(nn + 1) * 512], H['ones_f32'][:, 0:1],
                H['x_fm'][:, ff * TMY + nn * 512: ff * TMY + (nn + 1) * 512],
                start=(ff == 0), stop=(ff == DC - 1)),
                incs=[], waits=[('sDVE', dve_after_x1[f])])
        for n in range(2):
            last = (n == 1)
            E.do('tensor', lambda e, ff=f, nn=n: e.matmul(
                H['pC'][0:1, nn * 512:(nn + 1) * 512], H['ones_f32'][:, 0:1],
                H['sqb'][:, nn * 512:(nn + 1) * 512],
                start=(ff == 0), stop=(ff == DC - 1)),
                incs=([('sPE', 1)] if last else []),
                waits=[('sACT', act_n)])
    pe_stats1 = E.c['sPE']

    def ln_finalize(first_waits):
        E.do('vector', lambda e: e.tensor_scalar(
            H['r0'].ap(), H['pD'][0:1, 0:TMY], 1.0 / D, None, AO.mult),
            incs=[('sDVE', 1)],
            waits=list(first_waits) + [('sDVE', E.c['sDVE'])])
        E.do('vector', lambda e: e.tensor_scalar(
            H['r1'].ap(), H['pC'][0:1, 0:TMY], 1.0 / D, None, AO.mult),
            incs=[('sDVE', 1)], waits=[('sDVE', E.c['sDVE'])])
        E.do('scalar', lambda e: e.activation(H['r2'].ap(), H['r0'].ap(), AF.Square),
             incs=[('sACT', 1)], waits=[('sDVE', E.c['sDVE'])])
        E.do('vector', lambda e: e.tensor_tensor(
            H['r1'].ap(), H['r1'].ap(), H['r2'].ap(), AO.subtract),
            incs=[('sDVE', 1)],
            waits=[('sACT', E.c['sACT']), ('sDVE', E.c['sDVE'])])
        E.do('scalar', lambda e: e.activation(
            H['r2'].ap(), H['r1'].ap(), AF.Sqrt, bias=H['minis'][0:1, 0:1]),
            incs=[('sACT', 1)], waits=[('sDVE', E.c['sDVE'])])
        E.do('vector', lambda e: e.reciprocal(H['r3'].ap(), H['r2'].ap()),
             incs=[('sDVE', 1)],
             waits=[('sACT', E.c['sACT']), ('sDVE', E.c['sDVE'])])
        E.do('vector', lambda e: e.scalar_tensor_tensor(
            H['r1'].ap(), H['r0'].ap(), -1.0, H['r3'].ap(), AO.mult, AO.mult),
            incs=[('sDVE', 1)], waits=[('sDVE', E.c['sDVE'])])
        dven = E.c['sDVE']
        for (row, bc) in (('r3', 'bcA'), ('r1', 'bcB')):
            for n in range(2):
                E.do('tensor', lambda e, rr=row, nn=n: e.matmul(
                    H['pD'][:, nn * 512:(nn + 1) * 512], H['ones_f32'][0:1, :],
                    H[rr][0:1, nn * 512:(nn + 1) * 512], start=True, stop=True),
                    incs=([('sPE', 1)] if n == 1 else []),
                    waits=([('sDVE', dven), ('sDVE', E.c['sDVE'])] if n == 0 else []))
            pen2 = E.c['sPE']
            E.do('vector', lambda e, b=bc: e.tensor_copy(H[b].ap(), H['pD'].ap()),
                 incs=[('sDVE', 1)],
                 waits=[('sPE', pen2), ('sDVE', E.c['sDVE'])])
        return E.c['sDVE']

    dve_ln1 = ln_finalize([('sPE', pe_stats1)])

    dve_after_xn = [0] * DC
    for f in range(DC):
        E.do('vector', lambda e, ff=f: e.tensor_tensor(
            H['x_fm'][:, ff * TMY:(ff + 1) * TMY],
            H['x_fm'][:, ff * TMY:(ff + 1) * TMY], H['bcA'].ap(), AO.mult),
            incs=[('sDVE', 1)],
            waits=[('sDVE', dve_ln1), ('sDVE', E.c['sDVE'])])
        E.do('vector', lambda e, ff=f: e.tensor_tensor(
            H['x_fm'][:, ff * TMY:(ff + 1) * TMY],
            H['x_fm'][:, ff * TMY:(ff + 1) * TMY], H['bcB'].ap(), AO.add),
            incs=[('sDVE', 1)], waits=[('sDVE', E.c['sDVE'])])
        dve_after_xn[f] = E.c['sDVE']
    for f in range(DC):
        E.do('scalar', lambda e, ff=f: e.activation(
            H['sqb'].ap(), H['x_fm'][:, ff * TMY:(ff + 1) * TMY], AF.Square),
            incs=[('sACT', 1)],
            waits=[('sDVE', dve_after_xn[f]), ('sPE', E.c['sPE'])])
        act_n = E.c['sACT']
        for n in range(2):
            last = (n == 1)
            E.do('tensor', lambda e, ff=f, nn=n: e.matmul(
                H['pC'][0:1, nn * 512:(nn + 1) * 512], H['ones_f32'][:, 0:1],
                H['sqb'][:, nn * 512:(nn + 1) * 512],
                start=(ff == 0), stop=(ff == DC - 1)),
                incs=([('sPE', 1)] if last else []),
                waits=[('sACT', act_n)])
    pe_rms = E.c['sPE']

    E.do('vector', lambda e: e.tensor_scalar(
        H['r1'].ap(), H['pC'][0:1, 0:TMY], 1.0 / D, None, AO.mult),
        incs=[('sDVE', 1)],
        waits=[('sPE', pe_rms), ('sDVE', E.c['sDVE'])])
    E.do('scalar', lambda e: e.activation(
        H['r2'].ap(), H['r1'].ap(), AF.Sqrt, bias=H['minis'][0:1, 0:1]),
        incs=[('sACT', 1)], waits=[('sDVE', E.c['sDVE'])])
    E.do('vector', lambda e: e.reciprocal(H['r3'].ap(), H['r2'].ap()),
         incs=[('sDVE', 1)],
         waits=[('sACT', E.c['sACT']), ('sDVE', E.c['sDVE'])])
    dven = E.c['sDVE']
    for n in range(2):
        E.do('tensor', lambda e, nn=n: e.matmul(
            H['pD'][:, nn * 512:(nn + 1) * 512], H['ones_f32'][0:1, :],
            H['r3'][0:1, nn * 512:(nn + 1) * 512], start=True, stop=True),
            incs=([('sPE', 1)] if n == 1 else []),
            waits=([('sDVE', dven)] if n == 0 else []))
    pen = E.c['sPE']
    E.do('vector', lambda e: e.tensor_copy(H['bcC'].ap(), H['pD'].ap()),
         incs=[('sDVE', 1)], waits=[('sPE', pen), ('sDVE', E.c['sDVE'])])
    dve_bcC = E.c['sDVE']

    dve_after_t = [0] * DC
    for f in range(DC):
        E.do('vector', lambda e, ff=f: e.tensor_tensor(
            H['sqb'].ap(), H['x_fm'][:, ff * TMY:(ff + 1) * TMY], H['bcC'].ap(),
            AO.mult),
            incs=[('sDVE', 1)],
            waits=[('sDVE', dve_bcC), ('sDVE', E.c['sDVE']),
                   ('sPE', pe_after_proj[DC - 1])])
        E.do('vector', lambda e, ff=f: e.tensor_scalar(
            H['t_sb'][:, ff * TMY:(ff + 1) * TMY], H['sqb'].ap(),
            H['nw_c'][:, ff:ff + 1], None, AO.mult),
            incs=[('sDVE', 1)], waits=[('sDVE', E.c['sDVE'])])
        dve_after_t[f] = E.c['sDVE']
    dve_t_done = dve_after_t[DC - 1]

    E.do('sync', lambda e: e.dma_start(out=H['xn_spill'].ap(), in_=H['x_fm'].ap()),
         incs=[('sSPL', 16)], waits=[('sDVE', dve_t_done)])

    E.do('sync', lambda e: e.dma_start(out=H['b2_sb'].ap(), in_=H['i_b2'].ap()),
         incs=[('sLD2', 16)], waits=[])
    E.do('sync', lambda e: e.dma_start(out=H['oneh'].ap(), in_=H['i_oneh'].ap()),
         incs=[('sLD2', 16)], waits=[('sDVE', dve_t_done)])
    n_ld2 = E.c['sLD2']

    for c in range(8):
        waits = [('sDVE', dve_t_done), ('sGLD', n_gp_loads)]
        for dc in range(DC):
            E.do('tensor', lambda e, cc=c, d=dc: e.matmul(
                H['pD'][:, cc * 16:(cc + 1) * 16],
                H['t_sb'][:, d * TMY + cc * 128: d * TMY + (cc + 1) * 128],
                H['gwT'][:, d * 16:(d + 1) * 16],
                start=(d == 0), stop=False),
                incs=[], waits=(waits if (dc == 0 and c == 0) else []))
        last = (c == 7)
        E.do('tensor', lambda e, cc=c: e.matmul(
            H['pD'][:, cc * 16:(cc + 1) * 16],
            H['ones_f32'][0:1, :], H['gb'].ap(),
            start=False, stop=True),
            incs=([('sPE', 1)] if last else []),
            waits=([('sLDc', n_sp_loads)] if c == 0 else []))
    pe_logits = E.c['sPE']
    E.do('vector', lambda e: e.tensor_copy(H['G_sb'].ap(), H['pD'][:, 0:128]),
         incs=[('sDVE', 1)], waits=[('sPE', pe_logits), ('sDVE', E.c['sDVE'])])

    for c in range(8):
        Lsl = (c * 16, (c + 1) * 16)
        E.do('vector', lambda e, s=Lsl: e.tensor_copy(
            H['Ltmp'].ap(), H['G_sb'][:, s[0]:s[1]]),
            incs=[('sDVE', 1)], waits=[('sDVE', E.c['sDVE'])])
        for it in range(4):
            E.do('vector', lambda e, sl=it: e.reduce_max(
                H['mq4'][:, sl:sl + 1], H['Ltmp'].ap(), axis=AX.X),
                incs=[('sDVE', 1)], waits=[('sDVE', E.c['sDVE'])])
            if it < 3:
                E.do('vector', lambda e, sl=it: e.tensor_scalar(
                    H['eqm'].ap(), H['Ltmp'].ap(), H['mq4'][:, sl:sl + 1], None,
                    AO.is_equal),
                    incs=[('sDVE', 1)], waits=[('sDVE', E.c['sDVE'])])
                E.do('vector', lambda e: e.scalar_tensor_tensor(
                    H['Ltmp'].ap(), H['eqm'].ap(), -BIG, H['Ltmp'].ap(),
                    AO.mult, AO.add),
                    incs=[('sDVE', 1)], waits=[('sDVE', E.c['sDVE'])])
        E.do('vector', lambda e, s=Lsl: e.tensor_scalar(
            H['eqm'].ap(), H['G_sb'][:, s[0]:s[1]], H['mq4'][:, 3:4], None,
            AO.is_ge),
            incs=[('sDVE', 1)], waits=[('sDVE', E.c['sDVE'])])
        E.do('vector', lambda e: e.tensor_scalar(
            H['mq4'][:, 4:5], H['mq4'][:, 0:1], -1.0, None, AO.mult),
            incs=[('sDVE', 1)], waits=[('sDVE', E.c['sDVE'])])
        dven = E.c['sDVE']
        E.do('scalar', lambda e, s=Lsl: e.activation(
            H['expv'].ap(), H['G_sb'][:, s[0]:s[1]], AF.Exp,
            bias=H['mq4'][:, 4:5]),
            incs=[('sACT', 1)], waits=[('sDVE', dven)])
        E.do('vector', lambda e: e.tensor_tensor(
            H['expv'].ap(), H['expv'].ap(), H['eqm'].ap(), AO.mult),
            incs=[('sDVE', 1)],
            waits=[('sACT', E.c['sACT']), ('sDVE', E.c['sDVE'])])
        E.do('vector', lambda e: e.reduce_sum(
            H['mq4'][:, 5:6], H['expv'].ap(), axis=AX.X),
            incs=[('sDVE', 1)], waits=[('sDVE', E.c['sDVE'])])
        E.do('vector', lambda e: e.reciprocal(H['mq4'][:, 6:7], H['mq4'][:, 5:6]),
             incs=[('sDVE', 1)], waits=[('sDVE', E.c['sDVE'])])
        E.do('vector', lambda e, s=Lsl: e.tensor_scalar(
            H['G_sb'][:, s[0]:s[1]], H['expv'].ap(), H['mq4'][:, 6:7], None,
            AO.mult),
            incs=[('sDVE', 1)], waits=[('sDVE', E.c['sDVE'])])
    dve_G = E.c['sDVE']

    for c in range(8):
        last = (c == 7)
        E.do('tensor', lambda e, cc=c: e.transpose(
            H['pC'][0:16, cc * 128:(cc + 1) * 128],
            H['G_sb'][:, cc * 16:(cc + 1) * 16], H['eye_f32'].ap()),
            incs=([('sPE', 1)] if last else []),
            waits=([('sDVE', dve_G)] if c == 0 else []))
    pen = E.c['sPE']
    E.do('vector', lambda e: e.tensor_copy(H['GT_sb'].ap(), H['pC'][0:16, 0:TMY]),
         incs=[('sDVE', 1)], waits=[('sPE', pen), ('sDVE', E.c['sDVE'])])
    dve_GT = E.c['sDVE']

    dve_after_dinit = [0] * DC
    for g in range(DC):
        pP = ['pA', 'pB'][g % 2]
        waits = [('sDVE', dve_GT), ('sLD2', n_ld2), ('sSPL', 16)]
        if g >= 2:
            waits.append(('sDVE', dve_after_dinit[g - 2]))
        for n in range(2):
            E.do('tensor', lambda e, gg=g, nn=n, ps=pP: e.matmul(
                H[ps][:, nn * 512:(nn + 1) * 512],
                H['b2_sb'][:, gg * 128:(gg + 1) * 128],
                H['GT_sb'][0:16, nn * 512:(nn + 1) * 512],
                start=True, stop=True),
                incs=([('sPE', 1)] if n == 1 else []),
                waits=(waits if n == 0 else []))
        pen = E.c['sPE']
        E.do('vector', lambda e, gg=g, ps=pP: e.tensor_copy(
            H['delta'][:, gg * TMY:(gg + 1) * TMY], H[ps].ap()),
            incs=[('sDVE', 1)], waits=[('sPE', pen), ('sDVE', E.c['sDVE'])])
        dve_after_dinit[g] = E.c['sDVE']

    wa_issued = [0, 0]
    wg_issued = [0, 0]
    pe_after_h = {}
    pe_after_w2 = {}
    dve_after_act = {}
    dve_after_dacc = {}
    pair_last = {0: dve_after_dinit[DC - 1], 1: dve_GT}

    def issue_wa(e_, c_, waits=()):
        p2 = (e_ * DC + c_) % 2
        E.do('gpsimd', lambda e, ee=e_, cc=c_, p=p2: e.dma_start(
            out=H['wa%d' % p].ap(), in_=H['i_w1a'][ee, cc, :, :]),
            incs=[('sWa%d' % p2, 16)], waits=list(waits))
        E.do('gpsimd', lambda e, ee=e_, cc=c_, p=p2: e.dma_start(
            out=H['wb%d' % p].ap(), in_=H['i_w1b'][ee, cc, :, :]),
            incs=[('sWa%d' % p2, 16)])
        wa_issued[p2] += 1

    def issue_wg(e_, g_, waits=()):
        p2 = (e_ * DC + g_) % 2
        E.do('gpsimd', lambda e, ee=e_, gg=g_, p=p2: e.dma_start(
            out=H['wg%d' % p].ap(), in_=H['i_w2'][ee, gg, :, :]),
            incs=[('sWg%d' % p2, 16)], waits=list(waits))
        wg_issued[p2] += 1

    w_init_waits = [('sPE', pe_after_proj[DC - 1]), ('sDVE', dve_GT),
                    ('sACT', E.c['sACT'])]
    issue_wa(0, 0, waits=w_init_waits)
    issue_wa(0, 1)
    issue_wg(0, 0, waits=w_init_waits)
    issue_wg(0, 1)

    for e_ in range(NE):
        waits = [('sDVE', dve_GT), ('sLD2', n_ld2), ('sDVE', pair_last[1])]
        for n in range(2):
            E.do('tensor', lambda e, ee=e_, nn=n: e.matmul(
                H['pD'][:, nn * 512:(nn + 1) * 512],
                H['oneh'][0:16, ee * 128:(ee + 1) * 128],
                H['GT_sb'][0:16, nn * 512:(nn + 1) * 512],
                start=True, stop=True),
                incs=([('sPE', 1)] if n == 1 else []),
                waits=(waits if n == 0 else []))
        pen = E.c['sPE']
        E.do('vector', lambda e: e.tensor_copy(H['Gbc'].ap(), H['pD'].ap()),
             incs=[('sDVE', 1)], waits=[('sPE', pen), ('sDVE', E.c['sDVE'])])
        pair_last[1] = E.c['sDVE']
        dve_gbc = E.c['sDVE']

        for c in range(DC):
            gi = e_ * DC + c
            p2 = gi % 2
            pair = c % 2
            pha = ['pA', 'pC'][pair]
            phb = ['pB', 'pD'][pair]
            waits = [('sWa%d' % p2, 32 * ((gi // 2) + 1)),
                     ('sDVE', dve_t_done),
                     ('sDVE', pair_last[pair]),
                     ('sDVE', dve_gbc)]
            for dc in range(DC):
                for n in range(2):
                    E.do('tensor', lambda e, d=dc, nn=n, p=p2, ph=pha: e.matmul(
                        H[ph][:, nn * 512:(nn + 1) * 512],
                        H['wa%d' % p][:, d * 128:(d + 1) * 128],
                        H['t_sb'][:, d * TMY + nn * 512: d * TMY + (nn + 1) * 512],
                        start=(d == 0), stop=(d == DC - 1)),
                        incs=[], waits=(waits if (dc == 0 and n == 0) else []))
            for dc in range(DC):
                for n in range(2):
                    last = (dc == DC - 1 and n == 1)
                    E.do('tensor', lambda e, d=dc, nn=n, p=p2, ph=phb: e.matmul(
                        H[ph][:, nn * 512:(nn + 1) * 512],
                        H['wb%d' % p][:, d * 128:(d + 1) * 128],
                        H['t_sb'][:, d * TMY + nn * 512: d * TMY + (nn + 1) * 512],
                        start=(d == 0), stop=(d == DC - 1)),
                        incs=([('sPE', 1)] if last else []), waits=[])
            pe_after_h[gi] = E.c['sPE']
            ngi = gi + 2
            if ngi < NE * DC:
                issue_wa(ngi // DC, ngi % DC, waits=[('sPE', pe_after_h[gi])])

            bcol = e_ * 16 + c
            E.do('vector', lambda e, ph=pha, bc=bcol: e.tensor_scalar(
                H['ew_a'].ap(), H[ph].ap(), H['b1a_c'][:, bc:bc + 1], LIMIT,
                AO.add, AO.min),
                incs=[('sDVE', 1)],
                waits=[('sPE', pe_after_h[gi]), ('sDVE', E.c['sDVE'])])
            dven = E.c['sDVE']
            E.do('scalar', lambda e: e.activation(
                H['ew_s'].ap(), H['ew_a'].ap(), AF.Sigmoid, scale=ALPHA),
                incs=[('sACT', 1)], waits=[('sDVE', dven)])
            E.do('vector', lambda e: e.tensor_tensor(
                H['ew_a'].ap(), H['ew_a'].ap(), H['ew_s'].ap(), AO.mult),
                incs=[('sDVE', 1)],
                waits=[('sACT', E.c['sACT']), ('sDVE', E.c['sDVE'])])
            E.do('vector', lambda e, ph=phb, bc=bcol: e.tensor_scalar(
                H['ew_s'].ap(), H[ph].ap(), H['b1b_c'][:, bc:bc + 1], LIMIT,
                AO.add, AO.min),
                incs=[('sDVE', 1)], waits=[('sDVE', E.c['sDVE'])])
            E.do('vector', lambda e: e.tensor_scalar(
                H['ew_s'].ap(), H['ew_s'].ap(), -LIMIT, 1.0, AO.max, AO.add),
                incs=[('sDVE', 1)], waits=[('sDVE', E.c['sDVE'])])
            E.do('vector', lambda e: e.tensor_tensor(
                H['ew_s'].ap(), H['ew_s'].ap(), H['Gbc'].ap(), AO.mult),
                incs=[('sDVE', 1)], waits=[('sDVE', E.c['sDVE'])])
            E.do('vector', lambda e, cc=c: e.tensor_tensor(
                H['actp'][:, cc * TMY:(cc + 1) * TMY],
                H['ew_a'].ap(), H['ew_s'].ap(), AO.mult),
                incs=[('sDVE', 1)], waits=[('sDVE', E.c['sDVE'])])
            dve_after_act[gi] = E.c['sDVE']
            pair_last[pair] = E.c['sDVE']

        for g in range(DC):
            gi = e_ * DC + g
            p2 = gi % 2
            pP = ['pA', 'pB'][g % 2]
            waits = [('sWg%d' % p2, 16 * ((gi // 2) + 1)),
                     ('sDVE', dve_after_act[e_ * DC + DC - 1]),
                     ('sDVE', pair_last[0])]
            if g >= 2:
                waits.append(('sDVE', dve_after_dacc[gi - 2]))
            for dc in range(DC):
                for n in range(2):
                    last = (dc == DC - 1 and n == 1)
                    E.do('tensor', lambda e, d=dc, nn=n, p=p2, ps=pP: e.matmul(
                        H[ps][:, nn * 512:(nn + 1) * 512],
                        H['wg%d' % p][:, d * 128:(d + 1) * 128],
                        H['actp'][:, d * TMY + nn * 512: d * TMY + (nn + 1) * 512],
                        start=(d == 0), stop=(d == DC - 1)),
                        incs=([('sPE', 1)] if last else []),
                        waits=(waits if (dc == 0 and n == 0) else []))
            pe_after_w2[gi] = E.c['sPE']
            ngi = gi + 2
            if ngi < NE * DC:
                issue_wg(ngi // DC, ngi % DC, waits=[('sPE', pe_after_w2[gi])])
            E.do('vector', lambda e, gg=g, ps=pP: e.tensor_tensor(
                H['delta'][:, gg * TMY:(gg + 1) * TMY],
                H['delta'][:, gg * TMY:(gg + 1) * TMY], H[ps].ap(), AO.add),
                incs=[('sDVE', 1)],
                waits=[('sPE', pe_after_w2[gi]), ('sDVE', E.c['sDVE'])])
            dve_after_dacc[gi] = E.c['sDVE']
            if g % 2 == 1:
                pair_last[0] = E.c['sDVE']

    dve_p5_done = E.c['sDVE']
    pe_p5_done = E.c['sPE']

    for f in range(DC):
        E.do('sync', lambda e, ff=f: e.dma_start(
            out=H['xn_rl_%d' % ff].ap(),
            in_=H['xn_spill'][:, ff * TMY:(ff + 1) * TMY]),
            incs=[('sXN', 16)],
            waits=[('sSPL', 16), ('sPE', pe_p5_done)])
    n_xn = E.c['sXN']

    dve_after_x2 = [0] * DC
    for f in range(DC):
        E.do('vector', lambda e, ff=f: e.scalar_tensor_tensor(
            H['delta'][:, ff * TMY:(ff + 1) * TMY],
            H['xn_rl_%d' % ff].ap(), 2.0,
            H['delta'][:, ff * TMY:(ff + 1) * TMY], AO.mult, AO.add),
            incs=[('sDVE', 1)],
            waits=[('sXN', n_xn), ('sDVE', dve_p5_done), ('sDVE', E.c['sDVE'])])
        dve_after_x2[f] = E.c['sDVE']

    for f in range(DC):
        E.do('scalar', lambda e, ff=f: e.activation(
            H['sqb'].ap(), H['delta'][:, ff * TMY:(ff + 1) * TMY], AF.Square),
            incs=[('sACT', 1)],
            waits=[('sDVE', dve_after_x2[f]), ('sPE', E.c['sPE'])])
        act_n = E.c['sACT']
        for n in range(2):
            E.do('tensor', lambda e, ff=f, nn=n: e.matmul(
                H['pD'][0:1, nn * 512:(nn + 1) * 512], H['ones_f32'][:, 0:1],
                H['delta'][:, ff * TMY + nn * 512: ff * TMY + (nn + 1) * 512],
                start=(ff == 0), stop=(ff == DC - 1)),
                incs=[], waits=[('sDVE', dve_after_x2[f])])
        for n in range(2):
            last = (n == 1)
            E.do('tensor', lambda e, ff=f, nn=n: e.matmul(
                H['pC'][0:1, nn * 512:(nn + 1) * 512], H['ones_f32'][:, 0:1],
                H['sqb'][:, nn * 512:(nn + 1) * 512],
                start=(ff == 0), stop=(ff == DC - 1)),
                incs=([('sPE', 1)] if last else []),
                waits=[('sACT', act_n)])
    pe_stats3 = E.c['sPE']

    dve_ln2 = ln_finalize([('sPE', pe_stats3)])

    dve_after_out = [0] * DC
    for f in range(DC):
        E.do('vector', lambda e, ff=f: e.tensor_tensor(
            H['delta'][:, ff * TMY:(ff + 1) * TMY],
            H['delta'][:, ff * TMY:(ff + 1) * TMY], H['bcA'].ap(), AO.mult),
            incs=[('sDVE', 1)],
            waits=[('sDVE', dve_ln2), ('sDVE', E.c['sDVE'])])
        E.do('vector', lambda e, ff=f: e.tensor_tensor(
            H['delta'][:, ff * TMY:(ff + 1) * TMY],
            H['delta'][:, ff * TMY:(ff + 1) * TMY], H['bcB'].ap(), AO.add),
            incs=[('sDVE', 1)], waits=[('sDVE', E.c['sDVE'])])
        dve_after_out[f] = E.c['sDVE']

    sty = 0
    dve_otcopy = {0: 0, 1: 0}
    for tk in range(8):
        for half in range(2):
            pk = ['pC', 'pD'][half]
            waits = [('sDVE', dve_after_out[DC - 1]),
                     ('sDVE', dve_otcopy[half])]
            for i in range(8):
                f = half * 8 + i
                last = (i == 7)
                E.do('tensor', lambda e, ff=f, tt=tk, ii=i, pkk=pk: e.transpose(
                    H[pkk][:, ii * 128:(ii + 1) * 128],
                    H['delta'][:, ff * TMY + tt * 128: ff * TMY + (tt + 1) * 128],
                    H['eye_f32'].ap()),
                    incs=([('sPE', 1)] if last else []),
                    waits=(waits if i == 0 else []))
            pen = E.c['sPE']
            E.do('vector', lambda e, hf=half, pkk=pk: e.tensor_copy(
                H['outtok'][:, hf * 1024:(hf + 1) * 1024], H[pkk].ap()),
                incs=[('sDVE', 1)],
                waits=[('sPE', pen), ('sDVE', E.c['sDVE']),
                       ('sSTy', 16 * sty)])
            dve_otcopy[half] = E.c['sDVE']
        dven = E.c['sDVE']
        E.do('sync', lambda e, tt=tk: e.dma_start(
            out=H['o_y'][tt * 128:(tt + 1) * 128, :], in_=H['outtok'].ap()),
            incs=[('sSTy', 16)], waits=[('sDVE', dven)])
        sty += 1

    if sel == 'sync':
        E.eng.wait_ge(H['sSTy'], 16 * sty)


def _build_nc():
    nc = bass.Bass('TRN2', target_bir_lowering=False)
    H = {}

    def din(name, shape, dtype=F32):
        H[name] = nc.dram_tensor(name, list(shape), dtype, kind='ExternalInput')

    din('i_x', (T, D))
    din('i_xmy', (TMY, D))
    din('i_wqkv', (128, 16 * 6 * 128))
    din('i_bqkv', (128, 6))
    din('i_wout', (16, 128, 16 * 128))
    din('i_bout', (128, 16))
    din('i_nw', (128, 16))
    din('i_gwT', (128, 256))
    din('i_gb', (1, 16))
    din('i_w1a', (NE, DC, 128, 16 * 128))
    din('i_w1b', (NE, DC, 128, 16 * 128))
    din('i_b1a', (128, 256))
    din('i_b1b', (128, 256))
    din('i_w2', (NE, DC, 128, 16 * 128))
    din('i_b2', (16, D))
    din('i_oneh', (16, 16 * 128))
    din('c_eye_f32', (128, 128))
    din('c_ones_f32', (128, 128))

    H['o_y'] = nc.dram_tensor('o_y', [TMY, D], F32, kind='ExternalOutput')

    H['cc_in'] = nc.dram_tensor('cc_in', [256, T], BF, kind='Internal')
    H['cc_out0'] = nc.dram_tensor('cc_out0', [NCORES * 128, T], BF, kind='Internal')
    H['cc_out1'] = nc.dram_tensor('cc_out1', [NCORES * 128, T], BF, kind='Internal')
    H['xn_spill'] = nc.dram_tensor('xn_spill', [128, DC * TMY], F32, kind='Internal')

    psize = nc.SBUF_PARTITION_SIZE_BYTES
    LAYOUT_BYTES = 210560
    region = nc.alloc_sbuf_tensor('layout_region', [128, LAYOUT_BYTES],
                                  mybir.dt.uint8)
    BASE = nc.lookup_mloc(region).addr
    assert BASE + LAYOUT_BYTES <= psize - 2 * KB, (BASE, psize)

    def sb_at(name, shape, dtype, offset):
        offset += BASE
        nbytes = int(np.prod(shape[1:])) * mybir.dt.size(dtype)
        assert offset % 32 == 0, (name, offset)
        assert offset + nbytes <= BASE + LAYOUT_BYTES, (name, offset, nbytes)
        H[name] = nc.alloc_sbuf_tensor_at(name, list(shape), dtype, offset=offset)

    cur = 0
    s_off = {}

    def sbs(name, shape, dtype):
        nonlocal cur
        nbytes = int(np.prod(shape[1:])) * mybir.dt.size(dtype)
        cur = (cur + 31) & ~31
        s_off[name] = cur
        sb_at(name, shape, dtype, cur)
        cur += nbytes

    sbs('eye_bf', (128, 128), BF)
    sbs('eye_f32', (128, 128), F32)
    sbs('ones_f32', (128, 128), F32)
    sbs('ones_bf', (128, 128), BF)
    sbs('bqkv_c', (128, 6), F32)
    sbs('bout_c', (128, 16), F32)
    sbs('nw_c', (128, 16), F32)
    sbs('b1a_c', (128, 256), F32)
    sbs('b1b_c', (128, 256), F32)
    sbs('gwT', (128, 256), BF)
    sbs('gb', (1, 16), F32)
    sbs('G_sb', (128, 128), F32)
    sbs('bcC', (128, 1024), F32)
    sbs('sqb', (128, 1024), F32)
    sbs('E0', (128, 1024), BF)
    sbs('E1', (128, 1024), BF)
    sbs('att0', (128, 1024), BF)
    sbs('att1', (128, 1024), BF)
    sbs('vstage', (128, 256), BF)
    sbs('cbias', (128, 2), F32)
    sbs('minis', (1, 16), F32)
    sbs('Ltmp', (128, 16), F32)
    sbs('eqm', (128, 16), F32)
    sbs('expv', (128, 16), F32)
    sbs('mq4', (128, 8), F32)
    sbs('actp', (128, DC * 1024), BF)

    sb_at('GT_sb', (16, 1024), F32, s_off['bcC'])
    sb_at('Gbc', (128, 1024), F32, s_off['sqb'])
    sb_at('sacc_d', (128, 1024), F32, s_off['actp'])
    sb_at('sacc_p', (128, 1024), F32, s_off['actp'] + 4 * KB)
    cur = (cur + 31) & ~31
    ew_off = cur
    sb_at('ew_a', (128, 1024), F32, ew_off)
    sb_at('ew_s', (128, 1024), F32, ew_off + 4 * KB)
    sb_at('outtok', (128, 2048), F32, ew_off)
    sb_at('rbc', (128, 1024), F32, ew_off)
    cur = ew_off + 8 * KB

    cur = (cur + 31) & ~31
    bc_off = cur
    sb_at('bcA', (128, 1024), F32, bc_off)
    sb_at('bcB', (128, 1024), F32, bc_off + 4 * KB)
    sb_at('oneh', (16, 16 * 128), F32, bc_off)
    cur = bc_off + 8 * KB

    S_END = (cur + 63) & ~63

    A = S_END
    sb_at('qA', (128, T), BF, A)
    sb_at('qB', (128, T), BF, A + 16 * KB)
    sb_at('kA', (128, T), BF, A + 32 * KB)
    sb_at('kB', (128, T), BF, A + 48 * KB)
    sb_at('x_fm', (128, DC * TMY), F32, A)
    sb_at('delta', (128, DC * TMY), F32, A)

    B = A + 64 * KB
    sb_at('vTA', (128, T), BF, B)
    sb_at('vTB', (128, T), BF, B + 16 * KB)
    sb_at('attn_all', (128, DC * TMY), BF, B)
    sb_at('t_sb', (128, DC * TMY), BF, B)

    C = B + 32 * KB
    sb_at('wqkv_sb', (128, 16 * 6 * 128), BF, C)
    for i in range(4):
        sb_at('wo%d' % i, (128, 2048), BF, C + i * 4 * KB)
    sb_at('wa0', (128, 2048), BF, C)
    sb_at('wa1', (128, 2048), BF, C + 4 * KB)
    sb_at('wb0', (128, 2048), BF, C + 8 * KB)
    sb_at('wb1', (128, 2048), BF, C + 12 * KB)
    sb_at('wg0', (128, 2048), BF, C + 16 * KB)
    sb_at('wg1', (128, 2048), BF, C + 20 * KB)
    sb_at('r0', (1, 1024), F32, C + 16 * KB)
    sb_at('r1', (1, 1024), F32, C + 20 * KB)

    Dz = C + 24 * KB
    sb_at('xtok0', (128, 2048), BF, Dz)
    sb_at('xtok1', (128, 2048), BF, Dz + 4 * KB)
    sb_at('xT', (128, 2048), BF, Dz + 8 * KB)
    sb_at('xrel0', (128, 2048), F32, Dz)
    sb_at('xrel1', (128, 2048), F32, Dz + 8 * KB)
    sb_at('b2_sb', (16, D), F32, Dz)
    sb_at('r2', (1, 1024), F32, Dz + 8 * KB)
    sb_at('r3', (1, 1024), F32, Dz + 12 * KB)

    D_END = Dz + 16 * KB
    assert D_END <= LAYOUT_BYTES, (LAYOUT_BYTES, D_END)
    xn_spots = [B + i * 4 * KB for i in range(8)]
    xn_spots += [C + i * 4 * KB for i in range(4)]
    xn_spots += [Dz, Dz + 4 * KB]
    e_off = (s_off['E0'] + 31) & ~31
    assert s_off['att1'] + 2 * KB - e_off >= 8 * KB - 64, (s_off,)
    xn_spots += [e_off, e_off + 4 * KB]
    for i, o in enumerate(xn_spots):
        sb_at('xn_rl_%d' % i, (128, 1024), F32, o)

    for p in ('pA', 'pB', 'pC', 'pD'):
        H[p] = nc.alloc_psum_tensor(p, [128, 1024], F32)

    for s in SEMS:
        H[s] = nc.alloc_semaphore(s)

    return nc, H


def build():
    nc, H = _build_nc()
    with nc.Block() as block:
        def mk(sel):
            def f(eng):
                _emit_program(Emitter(nc, H, sel, eng))
            return f
        block.tensor(mk('tensor'))
        block.scalar(mk('scalar'))
        block.vector(mk('vector'))
        block.sync(mk('sync'))
        block.gpsimd(mk('gpsimd'))
    return nc


_NC_CACHE = None


def _get_nc():
    global _NC_CACHE
    if _NC_CACHE is None:
        _NC_CACHE = build()
    return _NC_CACHE


def _host_prep(inputs):
    x = np.asarray(inputs['L0_attn_x'], dtype=np.float32).reshape(T, D)
    w_qkv = np.asarray(inputs['L0_attn_w_qkv'], dtype=np.float32)
    b_qkv = np.asarray(inputs['L0_attn_b_qkv'], dtype=np.float32)
    w_out = np.asarray(inputs['L0_attn_w_out'], dtype=np.float32)
    b_out = np.asarray(inputs['L0_attn_b_out'], dtype=np.float32)
    norm_w = np.asarray(inputs['L0_ffn_norm_w'], dtype=np.float32)
    gate_w = np.asarray(inputs['L0_ffn_gate_w'], dtype=np.float32)
    gate_b = np.asarray(inputs['L0_ffn_gate_b'], dtype=np.float32)
    w1 = np.asarray(inputs['L0_ffn_w1'], dtype=np.float32)
    b1 = np.asarray(inputs['L0_ffn_b1'], dtype=np.float32)
    w2 = np.asarray(inputs['L0_ffn_w2'], dtype=np.float32)
    b2 = np.asarray(inputs['L0_ffn_b2'], dtype=np.float32)

    shared = {}
    shared['i_x'] = np.ascontiguousarray(x)
    shared['i_wout'] = np.ascontiguousarray(
        w_out.reshape(16, 128, 16, 128).transpose(2, 1, 0, 3).reshape(16, 128, 2048))
    shared['i_bout'] = np.ascontiguousarray(b_out.reshape(16, 128).T)
    shared['i_nw'] = np.ascontiguousarray(norm_w.reshape(16, 128).T)
    shared['i_gwT'] = np.ascontiguousarray(
        gate_w.T.reshape(16, 128, 16).transpose(1, 0, 2).reshape(128, 256))
    shared['i_gb'] = np.ascontiguousarray(gate_b.reshape(1, 16))

    def blk_w1(wh):
        whT = wh.transpose(0, 2, 1)
        return np.ascontiguousarray(
            whT.reshape(NE, 16, 128, 16, 128).transpose(0, 3, 2, 1, 4)
               .reshape(NE, 16, 128, 2048))

    shared['i_w1a'] = blk_w1(w1[:, 0::2, :])
    shared['i_w1b'] = blk_w1(w1[:, 1::2, :])
    w2T = w2.transpose(0, 2, 1)
    shared['i_w2'] = np.ascontiguousarray(
        w2T.reshape(NE, 16, 128, 16, 128).transpose(0, 3, 2, 1, 4)
           .reshape(NE, 16, 128, 2048))

    def col16(v):
        return np.ascontiguousarray(
            v.reshape(16, 16, 128).transpose(2, 0, 1).reshape(128, 256))

    shared['i_b1a'] = col16(b1[:, 0::2])
    shared['i_b1b'] = col16(b1[:, 1::2])
    shared['i_b2'] = np.ascontiguousarray(b2)

    oneh = np.zeros((16, 16 * 128), dtype=np.float32)
    for e in range(16):
        oneh[e, e * 128:(e + 1) * 128] = 1.0
    shared['i_oneh'] = oneh
    shared['c_eye_f32'] = np.eye(128, dtype=np.float32)
    shared['c_ones_f32'] = np.ones((128, 128), dtype=np.float32)

    in_maps = []
    for i in range(NCORES):
        m = dict(shared)
        m['i_xmy'] = np.ascontiguousarray(x[i * TMY:(i + 1) * TMY, :])
        cols = np.concatenate([np.arange(256 * i, 256 * (i + 1)),
                               2048 + np.arange(256 * i, 256 * (i + 1)),
                               4096 + np.arange(256 * i, 256 * (i + 1))])
        wq = w_qkv[:, cols]
        m['i_wqkv'] = np.ascontiguousarray(
            wq.reshape(16, 128, 6, 128).transpose(1, 0, 2, 3).reshape(128, 16 * 6 * 128))
        bq = b_qkv[cols]
        m['i_bqkv'] = np.ascontiguousarray(bq.reshape(6, 128).T)
        in_maps.append(m)
    return in_maps


def kernel(**inputs):
    nc = _get_nc()
    in_maps = _host_prep(inputs)
    res = run_bass_kernel_spmd(nc, in_maps, core_ids=list(range(NCORES)))
    y = np.concatenate([np.asarray(r['o_y']) for r in res.results], axis=0)
    return y.reshape(1, T, D).astype(np.float32)
